# version 01
# baseline (speedup 1.0000x reference)
import numpy as np
import ml_dtypes
import concourse.bass as bass
import concourse.mybir as mybir
from concourse.bass_utils import run_bass_kernel_spmd

F32 = mybir.dt.float32
BF16 = mybir.dt.bfloat16
AF = mybir.ActivationFunctionType
ALU = mybir.AluOpType
AX = mybir.AxisListType

S = 2048
DM = 4096
NT = 16
QSCALE = 1.0 / float(np.sqrt(128.0))
BF = ml_dtypes.bfloat16



class Sem:
    __slots__ = ("h", "n")

    def __init__(self, nc, name):
        self.h = nc.alloc_semaphore(name)
        self.n = 0


class Ctx:

    ENGINES = ("sync", "gpsimd", "scalar", "vector", "tensor")

    def __init__(self, nc):
        self.nc = nc
        self.st = {e: [] for e in self.ENGINES}

    def sem(self, name):
        return Sem(self.nc, name)

    def op(self, engine, body, waits=(), inc=None, inc_by=1):
        w = [(s.h, int(v)) for (s, v) in waits if v > 0]
        if inc is not None:
            inc.n += inc_by

        def thunk(eng, w=w, body=body, inc=inc, inc_by=inc_by):
            for h, v in w:
                eng.wait_ge(h, v)
            ins = body(eng)
            if inc is not None:
                ins.then_inc(inc.h, inc_by)
        self.st[engine].append(thunk)

    def waits(self, engine, waits):
        w = [(s.h, int(v)) for (s, v) in waits if v > 0]

        def thunk(eng, w=w):
            for h, v in w:
                eng.wait_ge(h, v)
        self.st[engine].append(thunk)

    def finalize(self):
        nc = self.nc
        with nc.Block() as block:
            block.sync(lambda e: [f(e) for f in self.st["sync"]])
            block.gpsimd(lambda e: [f(e) for f in self.st["gpsimd"]])
            block.scalar(lambda e: [f(e) for f in self.st["scalar"]])
            block.vector(lambda e: [f(e) for f in self.st["vector"]])
            block.tensor(lambda e: [f(e) for f in self.st["tensor"]])
        return nc



def emit_ln(ctx, x_ap, xlnT, buf, sems, ntiles=16, tok_base=0):
    (xt, xlb, stats, mv, stdv, rstd, negmr, eps_t, tr_ps, identbf) = (
        buf["xt"], buf["xlb"], buf["stats"], buf["mv"], buf["stdv"],
        buf["rstd"], buf["negmr"], buf["eps_t"], buf["tr_ps"], buf["identbf"])
    ld_x, v_mv, s_std, v_nm, s_xln, t_tr, s_trv, init = (
        sems["ld_x"], sems["v_mv"], sems["s_std"], sems["v_nm"],
        sems["s_xln"], sems["t_tr"], sems["s_trv"], sems["init"])

    for th in range(ntiles):
        tok0 = tok_base + th * 128
        ctx.op("sync", lambda e, t=tok0: e.dma_start(xt[:, :], x_ap[t:t + 128, :]),
               waits=[(s_xln, s_xln.n)], inc=ld_x, inc_by=16)
        for i in range(8):
            ctx.op("vector",
                   lambda e, i=i: e.bn_stats(stats[:, 6 * i:6 * i + 6],
                                             xt[:, 512 * i:512 * i + 512]),
                   waits=([(ld_x, ld_x.n)] if i == 0 else []))
        ctx.op("vector", lambda e: e.bn_aggr(mv[:, :], stats[:, :]),
               waits=[(s_std, s_std.n)], inc=v_mv)
        ctx.op("scalar",
               lambda e: e.activation(stdv[:, :], mv[:, 1:2], AF.Sqrt,
                                      bias=eps_t[:, :]),
               waits=[(v_mv, v_mv.n), (init, 2)], inc=s_std)
        ctx.op("vector", lambda e: e.reciprocal(rstd[:, :], stdv[:, :]),
               waits=[(s_std, s_std.n), (s_xln, s_xln.n)])
        ctx.op("vector", lambda e: e.drain())
        ctx.op("vector",
               lambda e: e.scalar_tensor_tensor(negmr[:, :], mv[:, 0:1], -1.0,
                                                rstd[:, :], ALU.mult, ALU.mult),
               inc=v_nm)
        ctx.op("scalar",
               lambda e: e.activation(xlb[:, :], xt[:, :], AF.Identity,
                                      bias=negmr[:, :], scale=rstd[:, :]),
               waits=[(v_nm, v_nm.n), (t_tr, t_tr.n)], inc=s_xln)
        for b in range(4):
            g = t_tr.n
            for j in range(8):
                r = b * 8 + j
                w = []
                if j == 0:
                    w.append((s_xln, s_xln.n))
                    if g >= 2:
                        w.append((s_trv, g - 1))
                ctx.op("tensor",
                       lambda e, g=g, j=j, r=r: e.transpose(
                           tr_ps[:, g % 2, 128 * j:128 * j + 128],
                           xlb[:, 128 * r:128 * r + 128], identbf[:, :]),
                       waits=w, inc=(t_tr if j == 7 else None))
            ctx.op("scalar",
                   lambda e, g=g, b=b, th=th: e.copy(
                       xlnT[:, 8 * b:8 * b + 8, 128 * th:128 * th + 128],
                       tr_ps[:, g % 2, :].rearrange("p (r t) -> p r t", r=8)),
                   waits=[(t_tr, t_tr.n)], inc=s_trv)



def emit_qkv(ctx, segs, xlnT, buf, sems, state):
    (ps, wring, bias_bf, ones1, qe, qr, rtmp, cosT, sinT) = (
        buf["ps"], buf["wring"], buf["bias_bf"], buf["ones1"], buf["qe"],
        buf["qr"], buf["rtmp"], buf["cosT"], buf["sinT"])
    g_w, t_mm, t_bank, s_ev, v_rope, st_out, g_b, init, s_trv = (
        sems["g_w"], sems["t_mm"], sems["t_bank"], sems["s_ev"],
        sems["v_rope"], sems["st_out"], sems["g_b"], sems["init"], sems["s_trv"])

    NS = wring.shape[1]
    first = True
    for half in (0, 1):
        for seg in segs:
            w_ap, b_ap, nfo, fo0, kind, out_ap, h0 = (
                seg["w"], seg["b"], seg["nfo"], seg["fo0"], seg["kind"],
                seg["out"], seg["h0"])
            for f in range(nfo):
                it = state["qkv_it"]
                state["qkv_it"] += 1
                c0 = (fo0 + f) * 512
                bb = it % 2
                ctx.op("gpsimd",
                       lambda e, bb=bb, b_ap=b_ap, c0=c0: e.dma_start(
                           bias_bf[:, bb, :], b_ap[:, c0:c0 + 512]),
                       waits=[(t_bank, (it - 1) * 8)], inc=g_b, inc_by=16)
                for r in range(32):
                    cg = state["qkv_chunk"]
                    state["qkv_chunk"] += 1
                    slot = cg % NS
                    ctx.op("gpsimd",
                           lambda e, slot=slot, r=r, w_ap=w_ap, c0=c0:
                           e.dma_start(wring[:, slot, :],
                                       w_ap[128 * r:128 * r + 128, c0:c0 + 512]),
                           waits=[(t_mm, cg - NS + 1)], inc=g_w, inc_by=16)
                    for tt in range(8):
                        w = [(g_w, g_w.n)] if tt == 0 else []
                        if r == 0:
                            if it >= 1:
                                w.append((s_ev, (it - 1) * 8 + tt + 1))
                            if first and tt == 0:
                                w.append((s_trv, s_trv.n))
                        tok = half * 1024 + tt * 128
                        ctx.op("tensor",
                               lambda e, slot=slot, r=r, tt=tt, tok=tok:
                               e.matmul(ps[:, tt, :],
                                        xlnT[:, r, tok:tok + 128],
                                        wring[:, slot, :],
                                        start=(r == 0), stop=False),
                               waits=w, inc=(t_mm if tt == 7 else None))
                first = False
                for tt in range(8):
                    w = [(g_b, g_b.n), (init, 2)] if tt == 0 else []
                    ctx.op("tensor",
                           lambda e, tt=tt, bb=bb: e.matmul(
                               ps[:, tt, :], ones1[:, :], bias_bf[:, bb, :],
                               start=False, stop=True),
                           waits=w, inc=t_bank)
                for tt in range(8):
                    tok0 = half * 1024 + tt * 128
                    eb = tt % 2
                    scale = QSCALE if kind == "q" else 1.0
                    w = [(t_bank, it * 8 + tt + 1)]
                    key = ("qe", eb)
                    if key in state:
                        w.append(state[key])
                    ctx.op("scalar",
                           lambda e, eb=eb, tt=tt, scale=scale: e.activation(
                               qe[:, eb, :], ps[:, tt, :], AF.Copy, scale=scale),
                           waits=w, inc=s_ev)
                    if kind in ("q", "k"):
                        state[key] = (v_rope, v_rope.n + 1)
                        rb = tt % 2
                        wr = [(s_ev, s_ev.n)]
                        keyr = ("qr", rb)
                        if keyr in state:
                            wr.append(state[keyr])

                        def hv(ap):
                            return ap.rearrange("p (h z d) -> p h z d", h=4, z=2)
                        x1 = hv(qe[:, eb, :])[:, :, 0, :]
                        x2 = hv(qe[:, eb, :])[:, :, 1, :]
                        y1 = hv(qr[:, rb, :])[:, :, 0, :]
                        y2 = hv(qr[:, rb, :])[:, :, 1, :]
                        cosv = cosT[:, tt + 8 * half, :].rearrange(
                            "p (h d) -> p h d", h=4)
                        sinv = sinT[:, tt + 8 * half, :].rearrange(
                            "p (h d) -> p h d", h=4)
                        t1 = rtmp[:, 0:256].rearrange("p (h d) -> p h d", h=4)
                        t2 = rtmp[:, 256:512].rearrange("p (h d) -> p h d", h=4)
                        ctx.op("vector", lambda e, x1=x1, sv=sinv, t1=t1:
                               e.tensor_tensor(t1, x1, sv, ALU.mult), waits=wr)
                        ctx.op("vector", lambda e, x2=x2, sv=sinv, t2=t2:
                               e.tensor_tensor(t2, x2, sv, ALU.mult))
                        ctx.op("vector", lambda e, x1=x1, cv=cosv, y1=y1:
                               e.tensor_tensor(y1, x1, cv, ALU.mult))
                        ctx.op("vector", lambda e, x2=x2, cv=cosv, y2=y2:
                               e.tensor_tensor(y2, x2, cv, ALU.mult))
                        ctx.op("vector", lambda e, y1=y1, t2=t2:
                               e.tensor_tensor(y1, y1, t2, ALU.subtract))
                        ctx.op("vector", lambda e, y2=y2, t1=t1:
                               e.tensor_tensor(y2, y2, t1, ALU.add), inc=v_rope)
                        state[keyr] = (st_out, st_out.n + 16)
                        src = qr[:, rb, :]
                        wd = [(v_rope, v_rope.n)]
                    else:
                        state[key] = (st_out, st_out.n + 16)
                        src = qe[:, eb, :]
                        wd = [(s_ev, s_ev.n)]
                    dst = out_ap[h0 + 4 * f:h0 + 4 * f + 4, tok0:tok0 + 128, :] \
                        .rearrange("h t d -> t h d")
                    ctx.op("sync",
                           lambda e, src=src, dst=dst: e.dma_start(
                               dst, src.rearrange("p (h d) -> p h d", h=4)),
                           waits=wd, inc=st_out, inc_by=16)


def alloc_qkv_bufs(nc):
    b = {}
    b["xlnT"] = nc.alloc_sbuf_tensor("xlnT", [128, 32, 2048], BF16)
    b["xt"] = nc.alloc_sbuf_tensor("xt", [128, DM], F32)
    b["xlb"] = nc.alloc_sbuf_tensor("xlb", [128, DM], BF16)
    b["stats"] = nc.alloc_sbuf_tensor("stats", [128, 48], F32)
    b["mv"] = nc.alloc_sbuf_tensor("mv", [128, 2], F32)
    b["stdv"] = nc.alloc_sbuf_tensor("stdv", [128, 1], F32)
    b["rstd"] = nc.alloc_sbuf_tensor("rstd", [128, 1], F32)
    b["negmr"] = nc.alloc_sbuf_tensor("negmr", [128, 1], F32)
    b["eps_t"] = nc.alloc_sbuf_tensor("eps_t", [128, 1], F32)
    b["identbf"] = nc.alloc_sbuf_tensor("identbf", [128, 128], BF16)
    b["cosT"] = nc.alloc_sbuf_tensor("cosT", [128, 16, 256], BF16)
    b["sinT"] = nc.alloc_sbuf_tensor("sinT", [128, 16, 256], BF16)
    b["wring"] = nc.alloc_sbuf_tensor("wring", [128, 8, 512], BF16)
    b["bias_bf"] = nc.alloc_sbuf_tensor("bias_bf", [1, 2, 512], BF16)
    b["ones1"] = nc.alloc_sbuf_tensor("ones1", [1, 128], BF16)
    b["qe"] = nc.alloc_sbuf_tensor("qe", [128, 2, 512], BF16)
    b["qr"] = nc.alloc_sbuf_tensor("qr", [128, 2, 512], BF16)
    b["rtmp"] = nc.alloc_sbuf_tensor("rtmp", [128, 512], BF16)
    b["ps"] = nc.alloc_psum_tensor("ps", [128, 8, 512], F32)
    b["tr_ps"] = b["ps"][:, 0:2, :].bitcast(BF16)
    return b


QKV_SEMS = ("ld_x", "v_mv", "s_std", "v_nm", "s_xln", "t_tr", "s_trv",
            "g_w", "t_mm", "t_bank", "s_ev", "v_rope", "st_out", "g_b",
            "init", "ld_c")


def emit_qkv_init(ctx, buf, ident, cos4, sin4):
    sems = {k: ctx.sem(k) for k in QKV_SEMS}
    init, ld_c = sems["init"], sems["ld_c"]
    ctx.op("sync", lambda e: e.dma_start(buf["identbf"][:, :], ident),
           inc=ld_c, inc_by=16)
    ctx.op("sync", lambda e: e.dma_start(
        buf["cosT"][:, :, :], cos4.rearrange("(t p) d -> p t d", p=128)),
        inc=ld_c, inc_by=16)
    ctx.op("sync", lambda e: e.dma_start(
        buf["sinT"][:, :, :], sin4.rearrange("(t p) d -> p t d", p=128)),
        inc=ld_c, inc_by=16)
    ctx.op("vector", lambda e: e.memset(buf["eps_t"][:, :], 1e-5), inc=init)
    ctx.op("vector", lambda e: e.memset(buf["ones1"][:, :], 1.0), inc=init)
    ctx.waits("tensor", [(ld_c, 48), (init, 2)])
    ctx.waits("vector", [(ld_c, 48)])
    return sems


def build_l0_qkv():
    nc = bass.Bass("TRN2", target_bir_lowering=False)
    x_in = nc.dram_tensor("x_in", [S, DM], F32, kind="ExternalInput").ap()
    qkv_w = nc.dram_tensor("qkv_w", [DM, 3 * DM], F32, kind="ExternalInput").ap()
    qkv_b = nc.dram_tensor("qkv_b", [1, 3 * DM], F32, kind="ExternalInput").ap()
    cos4 = nc.dram_tensor("cos4", [S, 256], BF16, kind="ExternalInput").ap()
    sin4 = nc.dram_tensor("sin4", [S, 256], BF16, kind="ExternalInput").ap()
    ident = nc.dram_tensor("ident", [128, 128], BF16, kind="ExternalInput").ap()
    Qt = nc.dram_tensor("Qt", [32, S, 128], BF16, kind="ExternalOutput").ap()
    Kt = nc.dram_tensor("Kt", [32, S, 128], BF16, kind="ExternalOutput").ap()
    Vt = nc.dram_tensor("Vt", [32, S, 128], BF16, kind="ExternalOutput").ap()

    buf = alloc_qkv_bufs(nc)
    ctx = Ctx(nc)
    sems = emit_qkv_init(ctx, buf, ident, cos4, sin4)

    emit_ln(ctx, x_in, buf["xlnT"], buf, sems)
    segs = [
        dict(w=qkv_w, b=qkv_b, nfo=8, fo0=0, kind="q", out=Qt, h0=0),
        dict(w=qkv_w, b=qkv_b, nfo=8, fo0=8, kind="k", out=Kt, h0=0),
        dict(w=qkv_w, b=qkv_b, nfo=8, fo0=16, kind="v", out=Vt, h0=0),
    ]
    state = {"qkv_it": 0, "qkv_chunk": 0}
    emit_qkv(ctx, segs, buf["xlnT"], buf, sems, state)
    ctx.waits("sync", [(sems["st_out"], sems["st_out"].n)])
    return ctx.finalize()



def host_consts():
    pos = np.arange(S, dtype=np.float64)
    inv = 1.0 / (10000.0 ** (np.arange(64, dtype=np.float64) / 64.0))
    ang = pos[:, None] * inv[None, :]
    cos = np.cos(ang)
    sin = np.sin(ang)
    cos4 = np.tile(cos, (1, 4)).astype(BF)
    sin4 = np.tile(sin, (1, 4)).astype(BF)
    ident = np.eye(128, dtype=BF)
    m128 = np.where(np.arange(128)[None, :] > np.arange(128)[:, None],
                    np.float32(-66504.0), np.float32(0.0)).astype(BF)
    return dict(cos4=cos4, sin4=sin4, ident=ident, m128=m128)





def emit_attn(ctx, H, kvg, Qt, Kt, Vt, ctxT_out, buf, sems):
    (qtok, ktok, vtok, qfeat, kfeat, ebuf, pbuf, ptS, ctxsb, nmax, rs, rinv,
     identbf, m128_sb, ps_s, pst, ctxps) = (
        buf["qtok"], buf["ktok"], buf["vtok"], buf["qfeat"], buf["kfeat"],
        buf["ebuf"], buf["pbuf"], buf["ptS"], buf["ctxsb"], buf["nmax"],
        buf["rs"], buf["rinv"], buf["identbf"], buf["m128_sb"], buf["ps_s"],
        buf["pst"], buf["ctxps"])
    ld_q, t_tr2, s_ptv, t_s, v_max, s_exp, v_nrm, t_pv, s_cv, st_o = (
        sems["ld_q"], sems["t_tr2"], sems["s_ptv"], sems["t_s"],
        sems["v_max"], sems["s_exp"], sems["v_nrm"], sems["t_pv"],
        sems["s_cv"], sems["st_o"])
    state = sems.setdefault("_state", {})
    ps_flat = ps_s.rearrange("p a b -> p (a b)")

    for h in range(H):
        hb = h % 2
        g = h // kvg
        new_kv = (h % kvg == 0)
        gb = g % 2
        wq = [state[("q", hb)]] if ("q", hb) in state else []
        ctx.op("sync", lambda e, h=h, hb=hb: e.dma_start(
            qtok[:, hb], Qt[h].rearrange("(t p) d -> p t d", p=128)),
            waits=wq, inc=ld_q, inc_by=16)
        if new_kv:
            wk = [state[("k", gb)]] if ("k", gb) in state else []
            ctx.op("sync", lambda e, g=g, gb=gb: e.dma_start(
                ktok[:, gb], Kt[g].rearrange("(t p) d -> p t d", p=128)),
                waits=wk, inc=ld_q, inc_by=16)
            wv = [state[("v", gb)]] if ("v", gb) in state else []
            ctx.op("sync", lambda e, g=g, gb=gb: e.dma_start(
                vtok[:, gb], Vt[g].rearrange("(t p) d -> p t d", p=128)),
                waits=wv, inc=ld_q, inc_by=16)
        ld_target = ld_q.n
        def tr_batches(src_tok, dst_feat, tiles):
            for b0 in range(0, tiles, 8):
                n = t_tr2.n
                for j in range(8):
                    kt = b0 + j
                    w = []
                    if j == 0:
                        w.append((ld_q, ld_target))
                        if n >= 2:
                            w.append((s_ptv, n - 1))
                    ctx.op("tensor", lambda e, n=n, j=j, kt=kt, s=src_tok:
                           e.transpose(pst[:, n % 2, 128 * j:128 * j + 128],
                                       s[:, kt, :], identbf[:, :]),
                           waits=w, inc=(t_tr2 if j == 7 else None))
                ctx.op("scalar", lambda e, n=n, b0=b0, d=dst_feat:
                       e.copy(d[:, b0:b0 + 8, :],
                              pst[:, n % 2, :].rearrange("p (r t) -> p r t", r=8)),
                       waits=[(t_tr2, t_tr2.n)], inc=s_ptv)
        tr_batches(qtok[:, hb], qfeat, 16)
        if new_kv:
            tr_batches(ktok[:, gb], kfeat, 16)
        state[("q", hb)] = (t_tr2, t_tr2.n)
        if new_kv:
            state[("k", gb)] = (t_tr2, t_tr2.n)
        qf = qfeat.rearrange("p a b -> p (a b)")
        kf = kfeat.rearrange("p a b -> p (a b)")

        for qt in range(16):
            nk = 128 * (qt + 1)
            nseg = (nk + 511) // 512
            eb = qt % 2
            for s in range(nseg):
                w0 = s * 512
                wdt = min(512, nk - w0)
                diag = (s == nseg - 1)
                w = []
                if s == 0:
                    w.append((s_exp, s_exp.n))
                    w.append((s_ptv, s_ptv.n))
                ctx.op("tensor", lambda e, s=s, w0=w0, wdt=wdt, qt=qt, diag=diag:
                       e.matmul(ps_s[:, s, 0:wdt], qf[:, 128 * qt:128 * qt + 128],
                                kf[:, w0:w0 + wdt], start=True, stop=not diag),
                       waits=w, inc=(None if diag else None))
            off = (128 * qt) % 512
            ctx.op("tensor", lambda e, s=nseg - 1, off=off:
                   e.matmul(ps_s[:, s, off:off + 128], identbf[:, :],
                            m128_sb[:, :], start=False, stop=True),
                   inc=t_s)
            ctx.op("vector", lambda e, nk=nk: e.reduce_max(
                nmax[:, :], ps_flat[:, 0:nk], axis=AX.X, negate=True),
                waits=[(t_s, t_s.n), (s_exp, s_exp.n)], inc=v_max)
            ctx.op("scalar", lambda e, nk=nk, eb=eb: e.activation(
                ebuf[:, eb, 0:nk], ps_flat[:, 0:nk], AF.Exp,
                bias=nmax[:, :], accum_out=rs[:, :]),
                waits=[(v_max, v_max.n)], inc=s_exp)
            ctx.op("vector", lambda e: e.reciprocal(rinv[:, :], rs[:, :]),
                   waits=[(s_exp, s_exp.n)])
            ctx.op("vector", lambda e: e.drain())
            ctx.op("vector", lambda e, nk=nk, eb=eb: e.tensor_scalar_mul(
                pbuf[:, eb, 0:nk], ebuf[:, eb, 0:nk], rinv[:, :]), inc=v_nrm)
            ntile = qt + 1
            for b0 in range(0, ntile, 8):
                n = t_tr2.n
                nb = min(8, ntile - b0)
                for j in range(nb):
                    kt = b0 + j
                    w = []
                    if j == 0:
                        w.append((v_nrm, v_nrm.n))
                        if n >= 2:
                            w.append((s_ptv, n - 1))
                    ctx.op("tensor", lambda e, n=n, j=j, kt=kt, eb=eb:
                           e.transpose(pst[:, n % 2, 128 * j:128 * j + 128],
                                       pbuf[:, eb, 128 * kt:128 * kt + 128],
                                       identbf[:, :]),
                           waits=w, inc=(t_tr2 if j == nb - 1 else None))
                ctx.op("scalar", lambda e, n=n, b0=b0, nb=nb:
                       e.copy(ptS[:, b0:b0 + nb, :],
                              pst[:, n % 2, 0:128 * nb].rearrange(
                                  "p (r t) -> p r t", r=nb)),
                       waits=[(t_tr2, t_tr2.n)], inc=s_ptv)
            cb = qt % 2
            for kt in range(ntile):
                w = []
                if kt == 0:
                    w.append((s_ptv, s_ptv.n))
                    if ("ctx", cb) in state:
                        w.append(state[("ctx", cb)])
                ctx.op("tensor", lambda e, kt=kt, cb=cb, gb=gb: e.matmul(
                    ctxps[:, cb, 0:128], vtok[:, gb, kt, :], ptS[:, kt, :],
                    start=(kt == 0), stop=(kt == ntile - 1)),
                    waits=w, inc=(t_pv if kt == ntile - 1 else None))
            state[("ctx", cb)] = (s_cv, s_cv.n + 1)
            ctx.op("scalar", lambda e, qt=qt, cb=cb, hb=hb: e.copy(
                ctxsb[:, hb, 128 * qt:128 * qt + 128], ctxps[:, cb, 0:128]),
                waits=[(t_pv, t_pv.n)], inc=s_cv)
        state[("v", gb)] = (t_pv, t_pv.n)
        wst = [(s_cv, s_cv.n)]
        if ("csb", hb) in state:
            wst.append(state[("csb", hb)])
        state[("csb", hb)] = (st_o, st_o.n + 16)
        ctx.op("sync", lambda e, h=h, hb=hb: e.dma_start(
            ctxT_out[h], ctxsb[:, hb, :]), waits=wst, inc=st_o, inc_by=16)


ATTN_SEMS = ("ld_q", "t_tr2", "s_ptv", "t_s", "v_max", "s_exp", "v_nrm",
             "t_pv", "s_cv", "st_o")


def alloc_attn_bufs(nc):
    b = {}
    b["qtok"] = nc.alloc_sbuf_tensor("qtok", [128, 2, 16, 128], BF16)
    b["ktok"] = nc.alloc_sbuf_tensor("ktok", [128, 2, 16, 128], BF16)
    b["vtok"] = nc.alloc_sbuf_tensor("vtok", [128, 2, 16, 128], BF16)
    b["qfeat"] = nc.alloc_sbuf_tensor("qfeat", [128, 16, 128], BF16)
    b["kfeat"] = nc.alloc_sbuf_tensor("kfeat", [128, 16, 128], BF16)
    b["ebuf"] = nc.alloc_sbuf_tensor("ebuf", [128, 2, 2048], BF16)
    b["pbuf"] = nc.alloc_sbuf_tensor("pbuf", [128, 2, 2048], BF16)
    b["ptS"] = nc.alloc_sbuf_tensor("ptS", [128, 16, 128], BF16)
    b["ctxsb"] = nc.alloc_sbuf_tensor("ctxsb", [128, 2, 2048], BF16)
    b["nmax"] = nc.alloc_sbuf_tensor("nmax", [128, 1], F32)
    b["rs"] = nc.alloc_sbuf_tensor("rs", [128, 1], F32)
    b["rinv"] = nc.alloc_sbuf_tensor("rinv", [128, 1], F32)
    b["identbf"] = nc.alloc_sbuf_tensor("identbf", [128, 128], BF16)
    b["m128_sb"] = nc.alloc_sbuf_tensor("m128_sb", [128, 128], BF16)
    b["ps_s"] = nc.alloc_psum_tensor("ps_s", [128, 4, 512], F32)
    b["pst"] = nc.alloc_psum_tensor("pst", [128, 2, 1024], BF16)
    b["ctxps"] = nc.alloc_psum_tensor("ctxps", [128, 2, 512], F32)
    return b


def build_attn(H, kvg):
    nc = bass.Bass("TRN2", target_bir_lowering=False)
    Hk = H // kvg
    Qt = nc.dram_tensor("Qt", [H, S, 128], BF16, kind="ExternalInput").ap()
    Kt = nc.dram_tensor("Kt", [Hk, S, 128], BF16, kind="ExternalInput").ap()
    Vt = nc.dram_tensor("Vt", [Hk, S, 128], BF16, kind="ExternalInput").ap()
    ident = nc.dram_tensor("ident", [128, 128], BF16, kind="ExternalInput").ap()
    m128 = nc.dram_tensor("m128", [128, 128], BF16, kind="ExternalInput").ap()
    ctxT = nc.dram_tensor("ctxT", [H, 128, S], BF16, kind="ExternalOutput").ap()

    buf = alloc_attn_bufs(nc)
    ctx = Ctx(nc)
    sems = {k: ctx.sem(k) for k in ATTN_SEMS}
    sems["ld_c"] = ctx.sem("ld_c")
    ctx.op("sync", lambda e: e.dma_start(buf["identbf"][:, :], ident),
           inc=sems["ld_c"], inc_by=16)
    ctx.op("sync", lambda e: e.dma_start(buf["m128_sb"][:, :], m128),
           inc=sems["ld_c"], inc_by=16)
    ctx.waits("tensor", [(sems["ld_c"], 32)])
    emit_attn(ctx, H, kvg, Qt, Kt, Vt, ctxT, buf, sems)
    ctx.waits("sync", [(sems["st_o"], sems["st_o"].n)])
    return ctx.finalize()



def emit_proj_resid(ctx, nr, lhsT_sb, w_ap, b_ap, xold_ap, xnew_ap, buf, sems,
                    state, extra=None):
    (ps, wring, bias_bf, ones1, xo, xn) = (
        buf["ps"], buf["wring"], buf["bias_bf"], buf["ones1"], buf["xo"],
        buf["xn"])
    g_w, t_mm, t_bank, s_ev, st_out, g_b, ld_xo, init = (
        sems["g_w"], sems["t_mm"], sems["t_bank"], sems["s_ev"],
        sems["st_out"], sems["g_b"], sems["ld_xo"], sems["init"])

    NC = w_ap.shape[1]
    NS = wring.shape[1]
    nfo = NC // 512
    for half in (0, 1):
        for f in range(nfo):
            it = state["pr_it"]
            state["pr_it"] += 1
            c0 = f * 512
            bb = it % 2
            ctx.op("gpsimd",
                   lambda e, bb=bb, c0=c0: e.dma_start(
                       bias_bf[:, bb, :], b_ap[:, c0:c0 + 512]),
                   waits=[(t_bank, (it - 1) * 8)], inc=g_b, inc_by=16)
            for r in range(nr):
                cg = state["pr_chunk"]
                state["pr_chunk"] += 1
                slot = cg % NS
                ctx.op("gpsimd",
                       lambda e, slot=slot, r=r, c0=c0: e.dma_start(
                           wring[:, slot, :],
                           w_ap[128 * r:128 * r + 128, c0:c0 + 512]),
                       waits=[(t_mm, cg - NS + 1)], inc=g_w, inc_by=16)
                for tt in range(8):
                    w = [(g_w, g_w.n)] if tt == 0 else []
                    if r == 0:
                        if it >= 1:
                            w.append((s_ev, (it - 1) * 8 + tt + 1))
                        if it == 0 and tt == 0 and "pr_ready" in state:
                            w.append(state["pr_ready"])
                    tok = half * 1024 + tt * 128
                    ctx.op("tensor",
                           lambda e, slot=slot, r=r, tt=tt, tok=tok:
                           e.matmul(ps[:, tt, :],
                                    lhsT_sb[:, r, tok:tok + 128],
                                    wring[:, slot, :],
                                    start=(r == 0), stop=False),
                           waits=w, inc=(t_mm if tt == 7 else None))
            for tt in range(8):
                w = [(g_b, g_b.n), (init, init.n)] if tt == 0 else []
                ctx.op("tensor",
                       lambda e, tt=tt, bb=bb: e.matmul(
                           ps[:, tt, :], ones1[:, :], bias_bf[:, bb, :],
                           start=False, stop=True),
                       waits=w, inc=t_bank)
            for tt in range(8):
                tok = half * 1024 + tt * 128
                xb = tt % 2
                w = [state[("xo", xb)]] if ("xo", xb) in state else []
                ctx.op("sync",
                       lambda e, xb=xb, tok=tok, c0=c0: e.dma_start(
                           xo[:, xb, :], xold_ap[tok:tok + 128, c0:c0 + 512]),
                       waits=w, inc=ld_xo, inc_by=16)
                state[("xo", xb)] = (s_ev, s_ev.n + 1)
                w = [(t_bank, it * 8 + tt + 1), (ld_xo, ld_xo.n)]
                if ("xn", xb) in state:
                    w.append(state[("xn", xb)])
                ctx.op("vector",
                       lambda e, tt=tt, xb=xb: e.tensor_add(
                           xn[:, xb, :], ps[:, tt, :], xo[:, xb, :]),
                       waits=w, inc=s_ev)
                state[("xn", xb)] = (st_out, st_out.n + 16)
                ctx.op("sync",
                       lambda e, xb=xb, tok=tok, c0=c0: e.dma_start(
                           xnew_ap[tok:tok + 128, c0:c0 + 512], xn[:, xb, :]),
                       waits=[(s_ev, s_ev.n)], inc=st_out, inc_by=16)


PROJ_SEMS = ("g_w", "t_mm", "t_bank", "s_ev", "st_out", "g_b", "ld_xo",
             "init", "ld_c")


def build_outproj():
    nc = bass.Bass("TRN2", target_bir_lowering=False)
    ctxT = nc.dram_tensor("ctxT", [32, 128, S], BF16, kind="ExternalInput").ap()
    w = nc.dram_tensor("w", [DM, DM], F32, kind="ExternalInput").ap()
    b = nc.dram_tensor("b", [1, DM], F32, kind="ExternalInput").ap()
    x_old = nc.dram_tensor("x_old", [S, DM], F32, kind="ExternalInput").ap()
    x_new = nc.dram_tensor("x_new", [S, DM], F32, kind="ExternalOutput").ap()

    buf = {}
    buf["cx"] = nc.alloc_sbuf_tensor("cx", [128, 32, 2048], BF16)
    buf["wring"] = nc.alloc_sbuf_tensor("wring", [128, 8, 512], BF16)
    buf["bias_bf"] = nc.alloc_sbuf_tensor("bias_bf", [1, 2, 512], BF16)
    buf["ones1"] = nc.alloc_sbuf_tensor("ones1", [1, 128], BF16)
    buf["xo"] = nc.alloc_sbuf_tensor("xo", [128, 2, 512], F32)
    buf["xn"] = nc.alloc_sbuf_tensor("xn", [128, 2, 512], F32)
    buf["ps"] = nc.alloc_psum_tensor("ps", [128, 8, 512], F32)

    ctx = Ctx(nc)
    sems = {k: ctx.sem(k) for k in PROJ_SEMS}
    ld_c = sems["ld_c"]
    for h in range(32):
        ctx.op("sync", lambda e, h=h: e.dma_start(buf["cx"][:, h, :], ctxT[h]),
               inc=ld_c, inc_by=16)
    ctx.op("vector", lambda e: e.memset(buf["ones1"][:, :], 1.0),
           inc=sems["init"])
    state = {"pr_it": 0, "pr_chunk": 0, "pr_ready": (ld_c, 32 * 16)}
    emit_proj_resid(ctx, 32, buf["cx"], w, b, x_old, x_new, buf, sems, state)
    ctx.waits("sync", [(sems["st_out"], sems["st_out"].n)])
    return ctx.finalize()



def emit_moe(ctx, x_ap, xnew_ap, rw_ap, rb_ap, eb_ap, buf, sems):
    (xt, xlb, stats, mv, stdv, rstd, negmr, eps_t, identbf, ones1,
     rw_bf, rwT, eb_bf, rb_bf, xlnL, se, scores, top8, masked_bf, wsum,
     nmax_l, rs_l, rinv_l, tkT_sb, tmp1, xn, ps) = (
        buf["xt"], buf["xlb"], buf["stats"], buf["mv"], buf["stdv"],
        buf["rstd"], buf["negmr"], buf["eps_t"], buf["identbf"], buf["ones1"],
        buf["rw_bf"], buf["rwT"], buf["eb_bf"], buf["rb_bf"], buf["xlnL"],
        buf["se"], buf["scores"], buf["top8"], buf["masked_bf"], buf["wsum"],
        buf["nmax_l"], buf["rs_l"], buf["rinv_l"], buf["tkT_sb"], buf["tmp1"],
        buf["xn"], buf["ps"])
    (ld_x, v_mv, s_std, v_nm, s_xln, t_tr, s_trv, t_lg, v_sm, s_exp, v_msk,
     t_tk, s_tk, t_mx, v_ev, st_out, g_c, init) = (
        sems["ld_x"], sems["v_mv"], sems["s_std"], sems["v_nm"], sems["s_xln"],
        sems["t_tr"], sems["s_trv"], sems["t_lg"], sems["v_sm"], sems["s_exp"],
        sems["v_msk"], sems["t_tk"], sems["s_tk"], sems["t_mx"], sems["v_ev"],
        sems["st_out"], sems["g_c"], sems["init"])

    pst = ps[:, 0:2, :].bitcast(BF16)
    ps_l = ps[:, 2, 0:64]
    tkps = ps[:, 3, :].bitcast(BF16)

    ctx.op("gpsimd", lambda e: e.dma_start(rw_bf[:, :], rw_ap,
                                           max_dma_last_dim=8192),
           inc=g_c, inc_by=16)
    ctx.op("gpsimd", lambda e: e.dma_start(eb_bf[:, :], eb_ap,
                                           max_dma_last_dim=8192),
           inc=g_c, inc_by=16)
    ctx.op("gpsimd", lambda e: e.dma_start(rb_bf[:, :], rb_ap),
           inc=g_c, inc_by=16)
    for b in range(4):
        g = t_tr.n
        for j in range(8):
            r = b * 8 + j
            w = []
            if j == 0:
                w.append((g_c, 48))
                if g >= 2:
                    w.append((s_trv, g - 1))
            ctx.op("tensor", lambda e, g=g, j=j, r=r: e.transpose(
                pst[0:128, g % 2, 64 * j:64 * j + 64],
                rw_bf[:, 128 * r:128 * r + 128], identbf[0:64, 0:64]),
                waits=w, inc=(t_tr if j == 7 else None))
        ctx.op("scalar", lambda e, g=g, b=b: e.copy(
            rwT[:, 8 * b:8 * b + 8, :],
            pst[:, g % 2, 0:512].rearrange("p (r t) -> p r t", r=8)),
            waits=[(t_tr, t_tr.n)], inc=s_trv)

    for th in range(16):
        tok0 = th * 128
        ctx.op("sync", lambda e, t=tok0: e.dma_start(xt[:, :], x_ap[t:t + 128, :]),
               waits=[(v_ev, v_ev.n)], inc=ld_x, inc_by=16)
        for i in range(8):
            ctx.op("vector", lambda e, i=i: e.bn_stats(
                stats[:, 6 * i:6 * i + 6], xt[:, 512 * i:512 * i + 512]),
                waits=([(ld_x, ld_x.n)] if i == 0 else []))
        ctx.op("vector", lambda e: e.bn_aggr(mv[:, :], stats[:, :]),
               waits=[(s_std, s_std.n)], inc=v_mv)
        ctx.op("scalar", lambda e: e.activation(
            stdv[:, :], mv[:, 1:2], AF.Sqrt, bias=eps_t[:, :]),
            waits=[(v_mv, v_mv.n), (init, 2)], inc=s_std)
        ctx.op("vector", lambda e: e.reciprocal(rstd[:, :], stdv[:, :]),
               waits=[(s_std, s_std.n), (s_xln, s_xln.n)])
        ctx.op("vector", lambda e: e.drain())
        ctx.op("vector", lambda e: e.scalar_tensor_tensor(
            negmr[:, :], mv[:, 0:1], -1.0, rstd[:, :], ALU.mult, ALU.mult),
            inc=v_nm)
        ctx.op("scalar", lambda e: e.activation(
            xlb[:, :], xt[:, :], AF.Identity, bias=negmr[:, :],
            scale=rstd[:, :]),
            waits=[(v_nm, v_nm.n), (t_tr, t_tr.n)], inc=s_xln)
        for b in range(4):
            g = t_tr.n
            for j in range(8):
                r = b * 8 + j
                w = []
                if j == 0:
                    w.append((s_xln, s_xln.n))
                    if g >= 2:
                        w.append((s_trv, g - 1))
                    if b == 0 and th > 0:
                        w.append((v_ev, v_ev.n))
                ctx.op("tensor", lambda e, g=g, j=j, r=r: e.transpose(
                    pst[:, g % 2, 128 * j:128 * j + 128],
                    xlb[:, 128 * r:128 * r + 128], identbf[:, :]),
                    waits=w, inc=(t_tr if j == 7 else None))
            ctx.op("scalar", lambda e, g=g, b=b: e.copy(
                xlnL[:, 8 * b:8 * b + 8, :],
                pst[:, g % 2, :].rearrange("p (r t) -> p r t", r=8)),
                waits=[(t_tr, t_tr.n)], inc=s_trv)
        for r in range(32):
            w = [(s_trv, s_trv.n)] if r == 0 else []
            if r == 0 and th > 0:
                w.append((s_exp, s_exp.n))
                w.append((v_ev, v_ev.n))
            ctx.op("tensor", lambda e, r=r: e.matmul(
                ps_l, xlnL[:, r, :], rwT[:, r, :], start=(r == 0), stop=False),
                waits=w)
        ctx.op("tensor", lambda e: e.matmul(
            ps_l, ones1[:, :], rb_bf[:, :], start=False, stop=True),
            waits=[(g_c, 48), (init, init.n)], inc=t_lg)
        ctx.op("vector", lambda e: e.reduce_max(
            nmax_l[:, :], ps_l, axis=AX.X, negate=True),
            waits=[(t_lg, t_lg.n), (s_exp, s_exp.n)], inc=v_sm)
        ctx.op("scalar", lambda e: e.activation(
            se[:, :], ps_l, AF.Exp, bias=nmax_l[:, :], accum_out=rs_l[:, :]),
            waits=[(v_sm, v_sm.n)], inc=s_exp)
        ctx.op("vector", lambda e: e.reciprocal(rinv_l[:, :], rs_l[:, :]),
               waits=[(s_exp, s_exp.n)])
        ctx.op("vector", lambda e: e.drain())
        ctx.op("vector", lambda e: e.tensor_scalar_mul(
            scores[:, :], se[:, :], rinv_l[:, :]))
        ctx.op("vector", lambda e: e.drain())
        ctx.op("vector", lambda e: e.max(top8[:, :], scores[:, :]))
        ctx.op("vector", lambda e: e.drain())
        ctx.op("vector", lambda e: e.scalar_tensor_tensor(
            masked_bf[:, :], scores[:, :], top8[:, 7:8], scores[:, :],
            ALU.is_ge, ALU.mult))
        ctx.op("vector", lambda e: e.reduce_sum(
            wsum[:, :], top8[:, :], axis=AX.X))
        ctx.op("vector", lambda e: e.drain(), inc=v_msk)
        ctx.op("tensor", lambda e: e.transpose(
            tkps[0:64, 0:128], masked_bf[:, :], identbf[:, :]),
            waits=[(v_msk, v_msk.n)], inc=t_tk)
        ctx.op("scalar", lambda e: e.copy(tkT_sb[:, :], tkps[0:64, 0:128]),
               waits=[(t_tk, t_tk.n)], inc=s_tk)
        for sgi in range(8):
            w = [(s_tk, s_tk.n)] if sgi == 0 else []
            ctx.op("tensor", lambda e, sgi=sgi: e.matmul(
                ps[:, sgi, :], tkT_sb[:, :],
                eb_bf[:, 512 * sgi:512 * sgi + 512], start=True, stop=True),
                waits=w, inc=t_mx)
        for sgi in range(8):
            w = [(t_mx, t_mx.n), (v_msk, v_msk.n)] if sgi == 0 else []
            ctx.op("vector", lambda e, sgi=sgi: e.tensor_scalar(
                tmp1[:, :], xlb[:, 512 * sgi:512 * sgi + 512], wsum[:, :], None,
                ALU.mult), waits=w)
            ctx.op("vector", lambda e, sgi=sgi: e.tensor_add(
                tmp1[:, :], tmp1[:, :], xt[:, 512 * sgi:512 * sgi + 512]))
            xb = sgi % 2
            wv = [state_xn] if (state_xn := sems.get("_xn_%d" % xb)) else []
            ctx.op("vector", lambda e, sgi=sgi, xb=xb: e.tensor_add(
                xn[:, xb, :], tmp1[:, :], ps[:, sgi, :]), waits=wv, inc=v_ev)
            sems["_xn_%d" % xb] = (st_out, st_out.n + 16)
            ctx.op("sync", lambda e, sgi=sgi, xb=xb, t=tok0: e.dma_start(
                xnew_ap[t:t + 128, 512 * sgi:512 * sgi + 512], xn[:, xb, :]),
                waits=[(v_ev, v_ev.n)], inc=st_out, inc_by=16)


MOE_SEMS = ("ld_x", "v_mv", "s_std", "v_nm", "s_xln", "t_tr", "s_trv",
            "t_lg", "v_sm", "s_exp", "v_msk", "t_tk", "s_tk", "t_mx",
            "v_ev", "st_out", "g_c", "init", "ld_c")


def build_moe():
    nc = bass.Bass("TRN2", target_bir_lowering=False)
    x_old = nc.dram_tensor("x_old", [S, DM], F32, kind="ExternalInput").ap()
    rw = nc.dram_tensor("rw", [64, DM], F32, kind="ExternalInput").ap()
    rb = nc.dram_tensor("rb", [1, 64], F32, kind="ExternalInput").ap()
    eb = nc.dram_tensor("eb", [64, DM], F32, kind="ExternalInput").ap()
    ident = nc.dram_tensor("ident", [128, 128], BF16, kind="ExternalInput").ap()
    x_new = nc.dram_tensor("x_new", [S, DM], F32, kind="ExternalOutput").ap()

    buf = {}
    buf["xt"] = nc.alloc_sbuf_tensor("xt", [128, DM], F32)
    buf["xlb"] = nc.alloc_sbuf_tensor("xlb", [128, DM], BF16)
    buf["stats"] = nc.alloc_sbuf_tensor("stats", [128, 48], F32)
    buf["mv"] = nc.alloc_sbuf_tensor("mv", [128, 2], F32)
    buf["stdv"] = nc.alloc_sbuf_tensor("stdv", [128, 1], F32)
    buf["rstd"] = nc.alloc_sbuf_tensor("rstd", [128, 1], F32)
    buf["negmr"] = nc.alloc_sbuf_tensor("negmr", [128, 1], F32)
    buf["eps_t"] = nc.alloc_sbuf_tensor("eps_t", [128, 1], F32)
    buf["identbf"] = nc.alloc_sbuf_tensor("identbf", [128, 128], BF16)
    buf["ones1"] = nc.alloc_sbuf_tensor("ones1", [1, 128], BF16)
    buf["rw_bf"] = nc.alloc_sbuf_tensor("rw_bf", [64, DM], BF16)
    buf["rwT"] = nc.alloc_sbuf_tensor("rwT", [128, 32, 64], BF16)
    buf["eb_bf"] = nc.alloc_sbuf_tensor("eb_bf", [64, DM], BF16)
    buf["rb_bf"] = nc.alloc_sbuf_tensor("rb_bf", [1, 64], BF16)
    buf["xlnL"] = nc.alloc_sbuf_tensor("xlnL", [128, 32, 128], BF16)
    buf["se"] = nc.alloc_sbuf_tensor("se", [128, 64], F32)
    buf["scores"] = nc.alloc_sbuf_tensor("scores", [128, 64], F32)
    buf["top8"] = nc.alloc_sbuf_tensor("top8", [128, 8], F32)
    buf["masked_bf"] = nc.alloc_sbuf_tensor("masked_bf", [128, 64], BF16)
    buf["wsum"] = nc.alloc_sbuf_tensor("wsum", [128, 1], F32)
    buf["nmax_l"] = nc.alloc_sbuf_tensor("nmax_l", [128, 1], F32)
    buf["rs_l"] = nc.alloc_sbuf_tensor("rs_l", [128, 1], F32)
    buf["rinv_l"] = nc.alloc_sbuf_tensor("rinv_l", [128, 1], F32)
    buf["tkT_sb"] = nc.alloc_sbuf_tensor("tkT_sb", [64, 128], BF16)
    buf["tmp1"] = nc.alloc_sbuf_tensor("tmp1", [128, 512], F32)
    buf["xn"] = nc.alloc_sbuf_tensor("xn", [128, 2, 512], F32)
    buf["ps"] = nc.alloc_psum_tensor("ps", [128, 8, 512], F32)

    ctx = Ctx(nc)
    sems = {k: ctx.sem(k) for k in MOE_SEMS}
    ctx.op("sync", lambda e: e.dma_start(buf["identbf"][:, :], ident),
           inc=sems["ld_c"], inc_by=16)
    ctx.op("vector", lambda e: e.memset(buf["eps_t"][:, :], 1e-5),
           inc=sems["init"])
    ctx.op("vector", lambda e: e.memset(buf["ones1"][:, :], 1.0),
           inc=sems["init"])
    ctx.waits("tensor", [(sems["ld_c"], 16), (sems["init"], 2)])
    emit_moe(ctx, x_old, x_new, rw, rb, eb, buf, sems)
    ctx.waits("sync", [(sems["st_out"], sems["st_out"].n)])
    return ctx.finalize()


def build_l1_qkv():
    nc = bass.Bass("TRN2", target_bir_lowering=False)
    x_in = nc.dram_tensor("x_in", [S, DM], F32, kind="ExternalInput").ap()
    Wq = nc.dram_tensor("Wq", [DM, DM], F32, kind="ExternalInput").ap()
    bq = nc.dram_tensor("bq", [1, DM], F32, kind="ExternalInput").ap()
    Wk = nc.dram_tensor("Wk", [DM, 1024], F32, kind="ExternalInput").ap()
    bk = nc.dram_tensor("bk", [1, 1024], F32, kind="ExternalInput").ap()
    Wv = nc.dram_tensor("Wv", [DM, 1024], F32, kind="ExternalInput").ap()
    bv = nc.dram_tensor("bv", [1, 1024], F32, kind="ExternalInput").ap()
    cos4 = nc.dram_tensor("cos4", [S, 256], BF16, kind="ExternalInput").ap()
    sin4 = nc.dram_tensor("sin4", [S, 256], BF16, kind="ExternalInput").ap()
    ident = nc.dram_tensor("ident", [128, 128], BF16, kind="ExternalInput").ap()
    Qt = nc.dram_tensor("Qt", [32, S, 128], BF16, kind="ExternalOutput").ap()
    Kt = nc.dram_tensor("Kt", [8, S, 128], BF16, kind="ExternalOutput").ap()
    Vt = nc.dram_tensor("Vt", [8, S, 128], BF16, kind="ExternalOutput").ap()

    buf = alloc_qkv_bufs(nc)
    ctx = Ctx(nc)
    sems = emit_qkv_init(ctx, buf, ident, cos4, sin4)

    emit_ln(ctx, x_in, buf["xlnT"], buf, sems)
    segs = [
        dict(w=Wq, b=bq, nfo=8, fo0=0, kind="q", out=Qt, h0=0),
        dict(w=Wk, b=bk, nfo=2, fo0=0, kind="k", out=Kt, h0=0),
        dict(w=Wv, b=bv, nfo=2, fo0=0, kind="v", out=Vt, h0=0),
    ]
    state = {"qkv_it": 0, "qkv_chunk": 0}
    emit_qkv(ctx, segs, buf["xlnT"], buf, sems, state)
    ctx.waits("sync", [(sems["st_out"], sems["st_out"].n)])
    return ctx.finalize()



def emit_lowrank_ffn(ctx, x_ap, y_ap, U1, V1, U2, V2, b1gT_ap, b1uT_ap, b2_ap,
                     buf, sems, dbg=None, stages=(1, 2, 3)):
    nc = ctx.nc
    ps = buf["ps"]
    xlnT = buf["xlnT"]
    PT_sb = buf["PT_sb"]

    def act(ci, c0=0, w=512):
        return xlnT[:, ci // 4, 512 * (ci % 4) + c0:512 * (ci % 4) + c0 + w]

    def zsb(tc, c0=0, w=1024):
        return xlnT[:, 24 + tc // 2, 1024 * (tc % 2) + c0:1024 * (tc % 2) + c0 + w]

    def zt(rc, c0=0, w=512):
        return xlnT[:, 26 + rc // 4, 512 * (rc % 4) + c0:512 * (rc % 4) + c0 + w]
    u1ring = buf["u1ring"]
    v1g = buf["v1g"]
    v1u = buf["v1u"]
    u2r = buf["u2r"]
    v2r = buf["v2r"]
    b2r = buf["b2r"]
    sg = buf["sg"]
    b1gT = buf["b1gT"]
    b1uT = buf["b1uT"]
    xo = buf["xo"]
    xn = buf["xn"]
    ones1 = buf["ones1"]
    identbf = buf["identbf"]
    pst = ps[:, 0:2, :].bitcast(BF16)

    (s_trv, t_tr, init) = (sems["s_trv"], sems["t_tr"], sems["init"])
    (g_u1, t_u1, s_pt, g_v1, t_gu, s_silu, v_act, g_u2, t_zc, t_z, s_zev,
     t_zt, s_ztv, g_v2, g_b2, t_v2, t_v2h, ld_xo, v_out, st_y, ld_b1) = (
        sems[k] for k in ("g_u1", "t_u1", "s_pt", "g_v1", "t_gu", "s_silu",
                          "v_act", "g_u2", "t_zc", "t_z", "s_zev", "t_zt",
                          "s_ztv", "g_v2", "g_b2", "t_v2", "t_v2h", "ld_xo",
                          "v_out", "st_y", "ld_b1"))

    ctx.op("sync", lambda e: e.dma_start(b1gT[:, :], b1gT_ap), inc=ld_b1,
           inc_by=16)
    ctx.op("sync", lambda e: e.dma_start(b1uT[:, :], b1uT_ap), inc=ld_b1,
           inc_by=16)

    ucnt = 0
    for cp in range(8):
        bank4 = (cp % 2) * 4
        ptv = None
        for r in range(32):
            slot = ucnt % 8
            ctx.op("gpsimd",
                   lambda e, slot=slot, r=r, cp=cp: e.dma_start(
                       u1ring[:, slot, :],
                       U1[128 * r:128 * r + 128, 128 * cp:128 * cp + 128]),
                   waits=[(t_u1, ucnt - 7)], inc=g_u1, inc_by=16)
            for sgi in range(4):
                w = [(g_u1, g_u1.n)] if sgi == 0 else []
                if r == 0 and sgi == 0:
                    if cp < 2:
                        w.append((s_trv, s_trv.n))
                    else:
                        w.append((s_pt, cp - 2 + 1))
                ctx.op("tensor",
                       lambda e, slot=slot, r=r, sgi=sgi, bank4=bank4: e.matmul(
                           ps[:, bank4 + sgi, :],
                           u1ring[:, slot, :],
                           xlnT[:, r, 512 * sgi:512 * sgi + 512],
                           start=(r == 0), stop=(r == 31)),
                       waits=w, inc=(t_u1 if sgi == 3 else None))
            ucnt += 1
        for sgi in range(4):
            ctx.op("scalar", lambda e, cp=cp, bank4=bank4, sgi=sgi: e.copy(
                PT_sb[:, cp, 512 * sgi:512 * sgi + 512],
                ps[:, bank4 + sgi, :]),
                waits=([(t_u1, t_u1.n)] if sgi == 0 else []),
                inc=(s_pt if sgi == 3 else None))

    if dbg is not None and "earlydump" in stages:
        ed_st = sems["st_y"]
        for cp in range(8):
            ctx.op("sync", lambda e, cp=cp: e.dma_start(
                dbg[3][cp], PT_sb[:, cp, :]),
                waits=[(s_pt, 8)], inc=ed_st, inc_by=16)
        sems["_edump"] = (ed_st, ed_st.n)

    for tg in range(4):
        t0 = 512 * tg
        for ci in range(86):
            bb = ci % 2
            w = [(t_gu, 2 * (86 * tg + ci) - 2)]
            if tg == 0 and ci == 0:
                w = [(t_tr, t_tr.n)]
            ctx.op("gpsimd", lambda e, bb=bb, ci=ci: e.dma_start(
                v1g[:, bb], V1[:, 128 * ci:128 * ci + 128].rearrange(
                    "(rc p) f -> p rc f", p=128)),
                waits=w, inc=g_v1, inc_by=16)
            ctx.op("gpsimd", lambda e, bb=bb, ci=ci: e.dma_start(
                v1u[:, bb], V1[:, 11008 + 128 * ci:11008 + 128 * ci + 128]
                .rearrange("(rc p) f -> p rc f", p=128)),
                inc=g_v1, inc_by=16)
            for rc in range(8):
                w = [(g_v1, g_v1.n)] if rc == 0 else []
                if rc == 0:
                    w.append((s_silu, s_silu.n))
                    w.append((s_pt, 8))
                    if tg == 0 and ci == 0 and "_edump" in sems:
                        w.append(sems["_edump"])
                    if tg > 0 and ci == 0 and 3 in stages:
                        w.append((s_ztv, s_ztv.n))
                ctx.op("tensor", lambda e, rc=rc, bb=bb, t0=t0: e.matmul(
                    ps[:, 0, :], v1g[:, bb, rc, :], PT_sb[:, rc, t0:t0 + 512],
                    start=(rc == 0), stop=(rc == 7)),
                    waits=w, inc=(t_gu if rc == 7 else None))
            for rc in range(8):
                w = []
                if rc == 0:
                    w.append((v_act, v_act.n))
                ctx.op("tensor", lambda e, rc=rc, bb=bb, t0=t0: e.matmul(
                    ps[:, 1, :], v1u[:, bb, rc, :], PT_sb[:, rc, t0:t0 + 512],
                    start=(rc == 0), stop=(rc == 7)),
                    waits=w, inc=(t_gu if rc == 7 else None))
            ctx.op("scalar", lambda e, ci=ci: e.activation(
                sg[:, :], ps[:, 0, :], AF.Silu, bias=b1gT[:, ci:ci + 1]),
                waits=[(t_gu, t_gu.n - 1), (ld_b1, 32), (v_act, v_act.n)],
                inc=s_silu)
            w = [(t_gu, t_gu.n), (s_silu, s_silu.n)]
            if tg == 0 and ci == 0:
                w.append((t_u1, t_u1.n))
            if tg > 0 and 2 in stages:
                w.append((t_zc, 86 * tg))
            ctx.op("vector", lambda e, ci=ci: e.scalar_tensor_tensor(
                act(ci), ps[:, 1, :], b1uT[:, ci:ci + 1], sg[:, :],
                ALU.add, ALU.mult), waits=w, inc=v_act)
        if 2 not in stages:
            continue
        for ci in range(86):
            bb = ci % 2
            ctx.op("gpsimd", lambda e, bb=bb, ci=ci: e.dma_start(
                u2r[:, bb], U2[128 * ci:128 * ci + 128, :]),
                waits=[(t_zc, 86 * tg + ci - 1)], inc=g_u2, inc_by=16)
            nmm = 0
            for tc in range(4):
                for rh in range(2):
                    w = [(g_u2, g_u2.n)] if nmm == 0 else []
                    if ci == 0 and nmm == 0:
                        w.append((s_silu, s_silu.n))
                        w.append((v_act, v_act.n))
                        if tg > 0 and 3 in stages:
                            w.append((v_out, v_out.n))
                        if 3 in stages:
                            w.append((s_zev, s_zev.n))
                    nmm += 1
                    ctx.op("tensor", lambda e, tc=tc, rh=rh, bb=bb, ci=ci:
                           e.matmul(ps[:, 2 * tc + rh, :],
                                    act(ci, 128 * tc, 128),
                                    u2r[:, bb, 512 * rh:512 * rh + 512],
                                    start=(ci == 0), stop=(ci == 85)),
                           waits=w,
                           inc=(t_zc if nmm == 8 else
                                (t_z if ci == 85 else None)))
        ctx.op("tensor", lambda e: e.nop(), waits=[(t_zc, 86 * (tg + 1))],
               inc=t_z)
        if 3 not in stages:
            continue
        for tc in range(4):
            for rh in range(2):
                ctx.op("scalar", lambda e, tc=tc, rh=rh: e.copy(
                    zsb(tc, 512 * rh, 512),
                    ps[:, 2 * tc + rh, :]),
                    waits=([(t_z, t_z.n)] if (tc == 0 and rh == 0) else []),
                    inc=(s_zev if rh == 1 else None))
        if dbg is not None and tg == 0:
            zd_, ad_ = dbg
            for tc in range(4):
                ctx.op("sync", lambda e, tc=tc: e.dma_start(
                    zd_[tc], Z_sb[:, tc, :]), waits=[(s_zev, s_zev.n)],
                    inc=sems["st_y"], inc_by=16)
            for cig in range(0, 86, 2):
                ctx.op("sync", lambda e, cig=cig: e.dma_start(
                    ad_[cig:cig + 2].rearrange("c p t -> p c t"),
                    act[:, cig:cig + 2, :]), waits=[(v_act, 86)],
                    inc=sems["st_y"], inc_by=16)
        for tc in range(4):
            n = t_zt.n
            for rc in range(8):
                w = []
                if rc == 0:
                    w.append((s_zev, s_zev.n))
                    if n >= 2:
                        w.append((s_ztv, s_ztv.n - 1))
                ctx.op("tensor", lambda e, n=n, tc=tc, rc=rc: e.transpose(
                    pst[:, n % 2, 128 * rc:128 * rc + 128],
                    zsb(tc, 128 * rc, 128), identbf[:, :]),
                    waits=w, inc=(t_zt if rc == 7 else None))
            for rc in range(8):
                ctx.op("scalar", lambda e, n=n, tc=tc, rc=rc: e.copy(
                    zt(rc, 128 * tc, 128),
                    pst[:, n % 2, 128 * rc:128 * rc + 128]),
                    waits=([(t_zt, t_zt.n)] if rc == 0 else []),
                    inc=(s_ztv if rc == 7 else None))
        for fi in range(8):
            for hf in range(2):
                w = [(t_v2, t_v2.n)] if hf == 0 else [(t_v2h, t_v2h.n)]
                ctx.op("gpsimd", lambda e, fi=fi, hf=hf: e.dma_start(
                    v2r[:, :, :],
                    V2[512 * hf:512 * hf + 512, 512 * fi:512 * fi + 512]
                    .rearrange("(rc p) f -> p rc f", p=128)),
                    waits=w, inc=g_v2, inc_by=16)
                for tc in range(4):
                    bank = 2 + tc
                    for rc in range(4):
                        w = [(g_v2, g_v2.n)] if (tc == 0 and rc == 0) else []
                        if hf == 0 and tc == 0 and rc == 0:
                            w.append((s_ztv, s_ztv.n))
                            w.append((v_out, v_out.n))
                        ctx.op("tensor", lambda e, bank=bank, rc=rc, tc=tc,
                               hf=hf: e.matmul(
                                   ps[:, bank, :],
                                   zt(4 * hf + rc, 128 * tc, 128),
                                   v2r[:, rc, :], start=(hf == 0 and rc == 0),
                                   stop=False),
                               waits=w,
                               inc=(t_v2h if (tc == 3 and rc == 3) else None))
            ctx.op("gpsimd", lambda e, fi=fi: e.dma_start(
                b2r[:, 0, :], b2_ap[:, 512 * fi:512 * fi + 512]),
                waits=[(t_v2, t_v2.n)], inc=g_b2, inc_by=16)
            for tc in range(4):
                bank = 2 + tc
                w = [(g_b2, g_b2.n), (init, init.n)] if tc == 0 else []
                ctx.op("tensor", lambda e, bank=bank: e.matmul(
                    ps[:, bank, :], ones1[:, :], b2r[:, 0, :],
                    start=False, stop=True),
                    waits=w, inc=t_v2)
                tok = t0 + 128 * tc
                xb = tc % 2
                if "noxo" in stages:
                    w = [(t_v2, t_v2.n)]
                    if "_ffnxn_s" in sems:
                        w.append(sems["_ffnxn_s"])
                    ctx.op("vector", lambda e, bank=bank: e.tensor_copy(
                        sg[:, :], ps[:, bank, :]), waits=w, inc=v_out)
                    sems["_ffnxn_s"] = (st_y, st_y.n + 16)
                    ctx.op("sync", lambda e, tok=tok, fi=fi: e.dma_start(
                        y_ap[tok:tok + 128, 512 * fi:512 * fi + 512], sg[:, :]),
                        waits=[(v_out, v_out.n)], inc=st_y, inc_by=16)
                else:
                    wx = [sems["_ffnxo_%d" % xb]] if ("_ffnxo_%d" % xb) in sems \
                        else [(s_pt, 8)]
                    ctx.op("sync", lambda e, xb=xb, tok=tok, fi=fi: e.dma_start(
                        xo[xb], x_ap[tok:tok + 128, 512 * fi:512 * fi + 512]),
                        waits=wx, inc=ld_xo, inc_by=16)
                    sems["_ffnxo_%d" % xb] = (v_out, v_out.n + 1)
                    w = [(t_v2, t_v2.n), (ld_xo, ld_xo.n)]
                    if ("_ffnxn_%d" % xb) in sems:
                        w.append(sems["_ffnxn_%d" % xb])
                    ctx.op("vector", lambda e, bank=bank, xb=xb: e.tensor_add(
                        xn[xb], ps[:, bank, :], xo[xb]),
                        waits=w, inc=v_out)
                    sems["_ffnxn_%d" % xb] = (st_y, st_y.n + 16)
                    ctx.op("sync", lambda e, xb=xb, tok=tok, fi=fi: e.dma_start(
                        y_ap[tok:tok + 128, 512 * fi:512 * fi + 512], xn[xb]),
                        waits=[(v_out, v_out.n)], inc=st_y, inc_by=16)


FFN_SEMS = ("ld_x", "v_mv", "s_std", "v_nm", "s_xln", "t_tr", "s_trv",
            "init", "ld_c", "g_u1", "t_u1", "s_pt", "g_v1", "t_gu", "s_silu",
            "v_act", "g_u2", "t_zc", "t_z", "s_zev", "t_zt", "s_ztv", "g_v2",
            "g_b2", "t_v2", "t_v2h", "ld_xo", "v_out", "st_y", "ld_b1")


def build_ffn1(dbg=False, stages=(1, 2, 3)):
    nc = bass.Bass("TRN2", target_bir_lowering=False)
    x_old = nc.dram_tensor("x_old", [S, DM], F32, kind="ExternalInput").ap()
    U1 = nc.dram_tensor("U1", [DM, 1024], F32, kind="ExternalInput").ap()
    V1 = nc.dram_tensor("V1", [1024, 22016], F32, kind="ExternalInput").ap()
    U2 = nc.dram_tensor("U2", [11008, 1024], F32, kind="ExternalInput").ap()
    V2 = nc.dram_tensor("V2", [1024, DM], F32, kind="ExternalInput").ap()
    b1g = nc.dram_tensor("b1g", [128, 86], F32, kind="ExternalInput").ap()
    b1u = nc.dram_tensor("b1u", [128, 86], F32, kind="ExternalInput").ap()
    b2 = nc.dram_tensor("b2", [1, DM], F32, kind="ExternalInput").ap()
    ident = nc.dram_tensor("ident", [128, 128], BF16, kind="ExternalInput").ap()
    y = nc.dram_tensor("y", [S, DM], F32, kind="ExternalOutput").ap()
    zd = None
    if dbg:
        zd = (nc.dram_tensor("zsbd", [4, 128, 1024], BF16,
                             kind="ExternalOutput").ap(),
              nc.dram_tensor("ztd", [8, 128, 512], BF16,
                             kind="ExternalOutput").ap(),
              nc.dram_tensor("actd", [86, 128, 512], BF16,
                             kind="ExternalOutput").ap(),
              nc.dram_tensor("ptd", [8, 128, 2048], BF16,
                             kind="ExternalOutput").ap())

    buf = {}
    buf["xlnT"] = nc.alloc_sbuf_tensor("xlnT", [128, 32, 2048], BF16)

    buf["xt"] = nc.alloc_sbuf_tensor("xt", [128, DM], F32)
    buf["xlb"] = nc.alloc_sbuf_tensor("xlb", [128, DM], BF16)
    xlbf = buf["xlb"][:, :]
    buf["v1g"] = xlbf[:, 0:2048].rearrange("p (b r f) -> p b r f", b=2, r=8)
    buf["v1u"] = xlbf[:, 2048:4096].rearrange("p (b r f) -> p b r f", b=2, r=8)

    buf["stats"] = nc.alloc_sbuf_tensor("stats", [128, 48], F32)
    buf["mv"] = nc.alloc_sbuf_tensor("mv", [128, 2], F32)
    buf["stdv"] = nc.alloc_sbuf_tensor("stdv", [128, 1], F32)
    buf["rstd"] = nc.alloc_sbuf_tensor("rstd", [128, 1], F32)
    buf["negmr"] = nc.alloc_sbuf_tensor("negmr", [128, 1], F32)
    buf["eps_t"] = nc.alloc_sbuf_tensor("eps_t", [128, 1], F32)
    buf["identbf"] = nc.alloc_sbuf_tensor("identbf", [128, 128], BF16)
    buf["ones1"] = nc.alloc_sbuf_tensor("ones1", [1, 128], BF16)
    buf["PT_sb"] = nc.alloc_sbuf_tensor("PT_sb", [128, 8, 2048], BF16)
    buf["u1ring"] = nc.alloc_sbuf_tensor("u1ring", [128, 8, 128], BF16)
    buf["u2r"] = nc.alloc_sbuf_tensor("u2r", [128, 2, 1024], BF16)
    buf["v2r"] = nc.alloc_sbuf_tensor("v2r", [128, 4, 512], BF16)
    buf["b2r"] = nc.alloc_sbuf_tensor("b2r", [1, 1, 512], BF16)
    xot = nc.alloc_sbuf_tensor("xot", [128, 2, 512], F32)
    xnt = nc.alloc_sbuf_tensor("xnt", [128, 2, 512], F32)
    buf["xo"] = (xot[:, 0, :], xot[:, 1, :])
    buf["xn"] = (xnt[:, 0, :], xnt[:, 1, :])
    buf["sg"] = nc.alloc_sbuf_tensor("sg", [128, 512], F32)
    buf["b1gT"] = nc.alloc_sbuf_tensor("b1gT", [128, 86], F32)
    buf["b1uT"] = nc.alloc_sbuf_tensor("b1uT", [128, 86], F32)

    buf["ps"] = nc.alloc_psum_tensor("ps", [128, 8, 512], F32)
    buf["tr_ps"] = buf["ps"][:, 0:2, :].bitcast(BF16)

    ctx = Ctx(nc)
    sems = {k: ctx.sem(k) for k in FFN_SEMS}
    ctx.op("sync", lambda e: e.dma_start(buf["identbf"][:, :], ident),
           inc=sems["ld_c"], inc_by=16)
    ctx.op("vector", lambda e: e.memset(buf["eps_t"][:, :], 1e-5),
           inc=sems["init"])
    ctx.op("vector", lambda e: e.memset(buf["ones1"][:, :], 1.0),
           inc=sems["init"])
    ctx.waits("tensor", [(sems["ld_c"], 16), (sems["init"], 2)])
    emit_ln(ctx, x_old, buf["xlnT"], buf, sems)
    emit_lowrank_ffn(ctx, x_old, y, U1, V1, U2, V2, b1g, b1u, b2, buf, sems,
                     dbg=None, stages=stages)
    if dbg:
        st_y = sems["st_y"]
        gates = [(st_y, st_y.n), (sems["v_act"], sems["v_act"].n),
                 (sems["t_zc"], sems["t_zc"].n), (sems["s_pt"], sems["s_pt"].n)]
        ctx.op("sync", lambda e: e.dma_start(
            zd[0][0].rearrange("a b -> a b"), buf["xlnT"][:, 24, 0:1024]) if False
            else e.nop(), waits=gates, inc=st_y)
        for cig in range(86):
            ctx.op("sync", lambda e, cig=cig: e.dma_start(
                zd[2][cig],
                buf["xlnT"][:, cig // 4, 512 * (cig % 4):512 * (cig % 4) + 512]),
                inc=st_y, inc_by=16)
        ctx.op("sync", lambda e: e.dma_start(
            zd[3].rearrange("c p t -> p c t"), buf["PT_sb"][:, :, :]),
            inc=st_y, inc_by=16)
    ctx.waits("sync", [(sems["st_y"], sems["st_y"].n)])
    return ctx.finalize()


_PROGS = {}


def _get_progs():
    if not _PROGS:
        _PROGS["l0qkv"] = build_l0_qkv()
        _PROGS["attn0"] = build_attn(32, 1)
        _PROGS["proj"] = build_outproj()
        _PROGS["moe"] = build_moe()
        _PROGS["l1qkv"] = build_l1_qkv()
        _PROGS["attn1"] = build_attn(32, 4)
        _PROGS["ffn1"] = build_ffn1()
    return _PROGS


def kernel(**inputs):
    p = _get_progs()
    cc = host_consts()
    B = 8
    f32 = np.float32
    cores = list(range(B))
    x = np.ascontiguousarray(np.asarray(inputs["L0_attn_hidden_states"], f32))
    qkv_w = np.asarray(inputs["L0_attn_qkv_w"], f32)
    qkv_b = np.asarray(inputs["L0_attn_qkv_b"], f32).reshape(1, -1)
    out_w = np.asarray(inputs["L0_attn_out_w"], f32)
    out_b = np.asarray(inputs["L0_attn_out_b"], f32).reshape(1, -1)
    rw = np.asarray(inputs["L0_ffn_router_weight"], f32)
    rb = np.asarray(inputs["L0_ffn_router_bias"], f32).reshape(1, -1)
    eb = np.asarray(inputs["L0_ffn_expert_bias"], f32)
    Wq = np.asarray(inputs["L1_attn_Wq"], f32)
    bq = np.asarray(inputs["L1_attn_bq"], f32).reshape(1, -1)
    Wk = np.asarray(inputs["L1_attn_Wk"], f32)
    bk = np.asarray(inputs["L1_attn_bk"], f32).reshape(1, -1)
    Wv = np.asarray(inputs["L1_attn_Wv"], f32)
    bv = np.asarray(inputs["L1_attn_bv"], f32).reshape(1, -1)
    Wo = np.asarray(inputs["L1_attn_Wo"], f32)
    bo = np.asarray(inputs["L1_attn_bo"], f32).reshape(1, -1)
    U1 = np.asarray(inputs["L1_ffn_U1"], f32)
    V1 = np.asarray(inputs["L1_ffn_V1"], f32)
    U2 = np.asarray(inputs["L1_ffn_U2"], f32)
    V2 = np.asarray(inputs["L1_ffn_V2"], f32)
    b1 = np.asarray(inputs["L1_ffn_b1"], f32)
    b2 = np.asarray(inputs["L1_ffn_b2"], f32).reshape(1, -1)
    b1g = np.ascontiguousarray(b1[:11008].reshape(86, 128).T)
    b1u = np.ascontiguousarray(b1[11008:].reshape(86, 128).T)

    def run(prog, maps):
        return run_bass_kernel_spmd(prog, maps, core_ids=cores).results

    r = run(p["l0qkv"], [dict(x_in=np.ascontiguousarray(x[b]), qkv_w=qkv_w,
                              qkv_b=qkv_b, cos4=cc["cos4"], sin4=cc["sin4"],
                              ident=cc["ident"]) for b in range(B)])
    r = [dict(b, **{}) for b in r]
    ra = run(p["attn0"], [dict(Qt=r[b]["Qt"], Kt=r[b]["Kt"], Vt=r[b]["Vt"],
                               ident=cc["ident"], m128=cc["m128"])
                          for b in range(B)])
    r1 = run(p["proj"], [dict(ctxT=ra[b]["ctxT"], w=out_w, b=out_b,
                              x_old=np.ascontiguousarray(x[b]))
                         for b in range(B)])
    r2 = run(p["moe"], [dict(x_old=r1[b]["x_new"], rw=rw, rb=rb, eb=eb,
                             ident=cc["ident"]) for b in range(B)])
    r3 = run(p["l1qkv"], [dict(x_in=r2[b]["x_new"], Wq=Wq, bq=bq, Wk=Wk, bk=bk,
                               Wv=Wv, bv=bv, cos4=cc["cos4"], sin4=cc["sin4"],
                               ident=cc["ident"]) for b in range(B)])
    ra1 = run(p["attn1"], [dict(Qt=r3[b]["Qt"], Kt=r3[b]["Kt"], Vt=r3[b]["Vt"],
                                ident=cc["ident"], m128=cc["m128"])
                           for b in range(B)])
    r4 = run(p["proj"], [dict(ctxT=ra1[b]["ctxT"], w=Wo, b=bo,
                              x_old=r2[b]["x_new"]) for b in range(B)])
    r5 = run(p["ffn1"], [dict(x_old=r4[b]["x_new"], U1=U1, V1=V1, U2=U2, V2=V2,
                              b1g=b1g, b1u=b1u, b2=b2, ident=cc["ident"])
                         for b in range(B)])
    return np.stack([r5[b]["y"] for b in range(B)], axis=0)



# version 02
# speedup vs baseline: 1.0013x; 1.0013x over previous
import numpy as np
import ml_dtypes
import concourse.bass as bass
import concourse.mybir as mybir
from concourse.bass_utils import run_bass_kernel_spmd

F32 = mybir.dt.float32
BF16 = mybir.dt.bfloat16
AF = mybir.ActivationFunctionType
ALU = mybir.AluOpType
AX = mybir.AxisListType

S = 2048
DM = 4096
NT = 16
QSCALE = 1.0 / float(np.sqrt(128.0))
BF = ml_dtypes.bfloat16



class Sem:
    __slots__ = ("h", "n")

    def __init__(self, nc, name):
        self.h = nc.alloc_semaphore(name)
        self.n = 0


class VSem:
    __slots__ = ("p", "base")

    def __init__(self, phys):
        self.p = phys
        self.base = phys.n

    @property
    def h(self):
        return self.p.h

    @property
    def n(self):
        return self.p.n - self.base

    @n.setter
    def n(self, v):
        self.p.n = self.base + v


class Ctx:

    ENGINES = ("sync", "gpsimd", "scalar", "vector", "tensor")

    def __init__(self, nc):
        self.nc = nc
        self.st = {e: [] for e in self.ENGINES}

    def sem(self, name):
        return Sem(self.nc, name)

    def phase_sems(self, names):
        if not hasattr(self, "_pool"):
            self._pool = []
        out = {}
        for i, k in enumerate(names):
            if i >= len(self._pool):
                self._pool.append(Sem(self.nc, "pool%d" % i))
            out[k] = VSem(self._pool[i])
        return out

    def op(self, engine, body, waits=(), inc=None, inc_by=1):
        w = [(s.h, int(v) + getattr(s, "base", 0)) for (s, v) in waits if v > 0]
        if inc is not None:
            inc.n += inc_by

        def thunk(eng, w=w, body=body, inc=inc, inc_by=inc_by):
            for h, v in w:
                eng.wait_ge(h, v)
            ins = body(eng)
            if inc is not None:
                ins.then_inc(inc.h, inc_by)
        self.st[engine].append(thunk)

    def waits(self, engine, waits):
        w = [(s.h, int(v) + getattr(s, "base", 0)) for (s, v) in waits if v > 0]

        def thunk(eng, w=w):
            for h, v in w:
                eng.wait_ge(h, v)
        self.st[engine].append(thunk)

    def finalize(self):
        nc = self.nc
        with nc.Block() as block:
            block.sync(lambda e: [f(e) for f in self.st["sync"]])
            block.gpsimd(lambda e: [f(e) for f in self.st["gpsimd"]])
            block.scalar(lambda e: [f(e) for f in self.st["scalar"]])
            block.vector(lambda e: [f(e) for f in self.st["vector"]])
            block.tensor(lambda e: [f(e) for f in self.st["tensor"]])
        return nc



def emit_ln(ctx, x_ap, xlnT, buf, sems, ntiles=16, tok_base=0):
    (xt, xlb, stats, mv, stdv, rstd, negmr, eps_t, tr_ps, identbf) = (
        buf["xt"], buf["xlb"], buf["stats"], buf["mv"], buf["stdv"],
        buf["rstd"], buf["negmr"], buf["eps_t"], buf["tr_ps"], buf["identbf"])
    ld_x, v_mv, s_std, v_nm, s_xln, t_tr, s_trv, init = (
        sems["ld_x"], sems["v_mv"], sems["s_std"], sems["v_nm"],
        sems["s_xln"], sems["t_tr"], sems["s_trv"], sems["init"])

    for th in range(ntiles):
        tok0 = tok_base + th * 128
        ctx.op("sync", lambda e, t=tok0: e.dma_start(xt[:, :], x_ap[t:t + 128, :]),
               waits=[(s_xln, s_xln.n)], inc=ld_x, inc_by=16)
        for i in range(8):
            ctx.op("vector",
                   lambda e, i=i: e.bn_stats(stats[:, 6 * i:6 * i + 6],
                                             xt[:, 512 * i:512 * i + 512]),
                   waits=([(ld_x, ld_x.n)] if i == 0 else []))
        ctx.op("vector", lambda e: e.bn_aggr(mv[:, :], stats[:, :]),
               waits=[(s_std, s_std.n)], inc=v_mv)
        ctx.op("scalar",
               lambda e: e.activation(stdv[:, :], mv[:, 1:2], AF.Sqrt,
                                      bias=eps_t[:, :]),
               waits=[(v_mv, v_mv.n), (init, 2)], inc=s_std)
        ctx.op("vector", lambda e: e.reciprocal(rstd[:, :], stdv[:, :]),
               waits=[(s_std, s_std.n), (s_xln, s_xln.n)])
        ctx.op("vector", lambda e: e.drain())
        ctx.op("vector",
               lambda e: e.scalar_tensor_tensor(negmr[:, :], mv[:, 0:1], -1.0,
                                                rstd[:, :], ALU.mult, ALU.mult),
               inc=v_nm)
        ctx.op("scalar",
               lambda e: e.activation(xlb[:, :], xt[:, :], AF.Identity,
                                      bias=negmr[:, :], scale=rstd[:, :]),
               waits=[(v_nm, v_nm.n), (t_tr, t_tr.n)], inc=s_xln)
        for b in range(4):
            g = t_tr.n
            for j in range(8):
                r = b * 8 + j
                w = []
                if j == 0:
                    w.append((s_xln, s_xln.n))
                    if g >= 2:
                        w.append((s_trv, g - 1))
                ctx.op("tensor",
                       lambda e, g=g, j=j, r=r: e.transpose(
                           tr_ps[:, g % 2, 128 * j:128 * j + 128],
                           xlb[:, 128 * r:128 * r + 128], identbf[:, :]),
                       waits=w, inc=(t_tr if j == 7 else None))
            ctx.op("scalar",
                   lambda e, g=g, b=b, th=th: e.copy(
                       xlnT[:, 8 * b:8 * b + 8, 128 * th:128 * th + 128],
                       tr_ps[:, g % 2, :].rearrange("p (r t) -> p r t", r=8)),
                   waits=[(t_tr, t_tr.n)], inc=s_trv)



def emit_qkv(ctx, segs, xlnT, buf, sems, state):
    (ps, wring, bias_bf, ones1, qe, qr, rtmp, cosT, sinT) = (
        buf["ps"], buf["wring"], buf["bias_bf"], buf["ones1"], buf["qe"],
        buf["qr"], buf["rtmp"], buf["cosT"], buf["sinT"])
    g_w, t_mm, t_bank, s_ev, v_rope, st_out, g_b, init, s_trv = (
        sems["g_w"], sems["t_mm"], sems["t_bank"], sems["s_ev"],
        sems["v_rope"], sems["st_out"], sems["g_b"], sems["init"], sems["s_trv"])

    NS = wring.shape[1]
    first = True
    for half in (0, 1):
        for seg in segs:
            w_ap, b_ap, nfo, fo0, kind, out_ap, h0 = (
                seg["w"], seg["b"], seg["nfo"], seg["fo0"], seg["kind"],
                seg["out"], seg["h0"])
            for f in range(nfo):
                it = state["qkv_it"]
                state["qkv_it"] += 1
                c0 = (fo0 + f) * 512
                bb = it % 2
                ctx.op("gpsimd",
                       lambda e, bb=bb, b_ap=b_ap, c0=c0: e.dma_start(
                           bias_bf[:, bb, :], b_ap[:, c0:c0 + 512]),
                       waits=[(t_bank, (it - 1) * 8)], inc=g_b, inc_by=16)
                for r in range(32):
                    cg = state["qkv_chunk"]
                    state["qkv_chunk"] += 1
                    slot = cg % NS
                    ctx.op("gpsimd",
                           lambda e, slot=slot, r=r, w_ap=w_ap, c0=c0:
                           e.dma_start(wring[:, slot, :],
                                       w_ap[128 * r:128 * r + 128, c0:c0 + 512]),
                           waits=[(t_mm, cg - NS + 1)], inc=g_w, inc_by=16)
                    for tt in range(8):
                        w = [(g_w, g_w.n)] if tt == 0 else []
                        if r == 0:
                            if it >= 1:
                                w.append((s_ev, (it - 1) * 8 + tt + 1))
                            if first and tt == 0:
                                w.append((s_trv, s_trv.n))
                        tok = half * 1024 + tt * 128
                        ctx.op("tensor",
                               lambda e, slot=slot, r=r, tt=tt, tok=tok:
                               e.matmul(ps[:, tt, :],
                                        xlnT[:, r, tok:tok + 128],
                                        wring[:, slot, :],
                                        start=(r == 0), stop=False),
                               waits=w, inc=(t_mm if tt == 7 else None))
                first = False
                for tt in range(8):
                    w = [(g_b, g_b.n), (init, 2)] if tt == 0 else []
                    ctx.op("tensor",
                           lambda e, tt=tt, bb=bb: e.matmul(
                               ps[:, tt, :], ones1[:, :], bias_bf[:, bb, :],
                               start=False, stop=True),
                           waits=w, inc=t_bank)
                for tt in range(8):
                    tok0 = half * 1024 + tt * 128
                    eb = tt % 2
                    scale = QSCALE if kind == "q" else 1.0
                    w = [(t_bank, it * 8 + tt + 1)]
                    key = ("qe", eb)
                    if key in state:
                        w.append(state[key])
                    ctx.op("scalar",
                           lambda e, eb=eb, tt=tt, scale=scale: e.activation(
                               qe[:, eb, :], ps[:, tt, :], AF.Copy, scale=scale),
                           waits=w, inc=s_ev)
                    if kind in ("q", "k"):
                        state[key] = (v_rope, v_rope.n + 1)
                        rb = tt % 2
                        wr = [(s_ev, s_ev.n)]
                        keyr = ("qr", rb)
                        if keyr in state:
                            wr.append(state[keyr])

                        def hv(ap):
                            return ap.rearrange("p (h z d) -> p h z d", h=4, z=2)
                        x1 = hv(qe[:, eb, :])[:, :, 0, :]
                        x2 = hv(qe[:, eb, :])[:, :, 1, :]
                        y1 = hv(qr[:, rb, :])[:, :, 0, :]
                        y2 = hv(qr[:, rb, :])[:, :, 1, :]
                        cosv = cosT[:, tt + 8 * half, :].rearrange(
                            "p (h d) -> p h d", h=4)
                        sinv = sinT[:, tt + 8 * half, :].rearrange(
                            "p (h d) -> p h d", h=4)
                        t1 = rtmp[:, 0:256].rearrange("p (h d) -> p h d", h=4)
                        t2 = rtmp[:, 256:512].rearrange("p (h d) -> p h d", h=4)
                        ctx.op("vector", lambda e, x1=x1, sv=sinv, t1=t1:
                               e.tensor_tensor(t1, x1, sv, ALU.mult), waits=wr)
                        ctx.op("vector", lambda e, x2=x2, sv=sinv, t2=t2:
                               e.tensor_tensor(t2, x2, sv, ALU.mult))
                        ctx.op("vector", lambda e, x1=x1, cv=cosv, y1=y1:
                               e.tensor_tensor(y1, x1, cv, ALU.mult))
                        ctx.op("vector", lambda e, x2=x2, cv=cosv, y2=y2:
                               e.tensor_tensor(y2, x2, cv, ALU.mult))
                        ctx.op("vector", lambda e, y1=y1, t2=t2:
                               e.tensor_tensor(y1, y1, t2, ALU.subtract))
                        ctx.op("vector", lambda e, y2=y2, t1=t1:
                               e.tensor_tensor(y2, y2, t1, ALU.add), inc=v_rope)
                        state[keyr] = (st_out, st_out.n + 16)
                        src = qr[:, rb, :]
                        wd = [(v_rope, v_rope.n)]
                    else:
                        state[key] = (st_out, st_out.n + 16)
                        src = qe[:, eb, :]
                        wd = [(s_ev, s_ev.n)]
                    dst = out_ap[h0 + 4 * f:h0 + 4 * f + 4, tok0:tok0 + 128, :] \
                        .rearrange("h t d -> t h d")
                    ctx.op("sync",
                           lambda e, src=src, dst=dst: e.dma_start(
                               dst, src.rearrange("p (h d) -> p h d", h=4)),
                           waits=wd, inc=st_out, inc_by=16)


def alloc_qkv_bufs(nc):
    b = {}
    b["xlnT"] = nc.alloc_sbuf_tensor("xlnT", [128, 32, 2048], BF16)
    b["xt"] = nc.alloc_sbuf_tensor("xt", [128, DM], F32)
    b["xlb"] = nc.alloc_sbuf_tensor("xlb", [128, DM], BF16)
    b["stats"] = nc.alloc_sbuf_tensor("stats", [128, 48], F32)
    b["mv"] = nc.alloc_sbuf_tensor("mv", [128, 2], F32)
    b["stdv"] = nc.alloc_sbuf_tensor("stdv", [128, 1], F32)
    b["rstd"] = nc.alloc_sbuf_tensor("rstd", [128, 1], F32)
    b["negmr"] = nc.alloc_sbuf_tensor("negmr", [128, 1], F32)
    b["eps_t"] = nc.alloc_sbuf_tensor("eps_t", [128, 1], F32)
    b["identbf"] = nc.alloc_sbuf_tensor("identbf", [128, 128], BF16)
    b["cosT"] = nc.alloc_sbuf_tensor("cosT", [128, 16, 256], BF16)
    b["sinT"] = nc.alloc_sbuf_tensor("sinT", [128, 16, 256], BF16)
    b["wring"] = nc.alloc_sbuf_tensor("wring", [128, 8, 512], BF16)
    b["bias_bf"] = nc.alloc_sbuf_tensor("bias_bf", [1, 2, 512], BF16)
    b["ones1"] = nc.alloc_sbuf_tensor("ones1", [1, 128], BF16)
    b["qe"] = nc.alloc_sbuf_tensor("qe", [128, 2, 512], BF16)
    b["qr"] = nc.alloc_sbuf_tensor("qr", [128, 2, 512], BF16)
    b["rtmp"] = nc.alloc_sbuf_tensor("rtmp", [128, 512], BF16)
    b["ps"] = nc.alloc_psum_tensor("ps", [128, 8, 512], F32)
    b["tr_ps"] = b["ps"][:, 0:2, :].bitcast(BF16)
    return b


QKV_SEMS = ("ld_x", "v_mv", "s_std", "v_nm", "s_xln", "t_tr", "s_trv",
            "g_w", "t_mm", "t_bank", "s_ev", "v_rope", "st_out", "g_b",
            "init", "ld_c")


def emit_qkv_init(ctx, buf, ident, cos4, sin4):
    sems = {k: ctx.sem(k) for k in QKV_SEMS}
    init, ld_c = sems["init"], sems["ld_c"]
    ctx.op("sync", lambda e: e.dma_start(buf["identbf"][:, :], ident),
           inc=ld_c, inc_by=16)
    ctx.op("sync", lambda e: e.dma_start(
        buf["cosT"][:, :, :], cos4.rearrange("(t p) d -> p t d", p=128)),
        inc=ld_c, inc_by=16)
    ctx.op("sync", lambda e: e.dma_start(
        buf["sinT"][:, :, :], sin4.rearrange("(t p) d -> p t d", p=128)),
        inc=ld_c, inc_by=16)
    ctx.op("vector", lambda e: e.memset(buf["eps_t"][:, :], 1e-5), inc=init)
    ctx.op("vector", lambda e: e.memset(buf["ones1"][:, :], 1.0), inc=init)
    ctx.waits("tensor", [(ld_c, 48), (init, 2)])
    ctx.waits("vector", [(ld_c, 48)])
    return sems


def build_l0_qkv():
    nc = bass.Bass("TRN2", target_bir_lowering=False)
    x_in = nc.dram_tensor("x_in", [S, DM], F32, kind="ExternalInput").ap()
    qkv_w = nc.dram_tensor("qkv_w", [DM, 3 * DM], F32, kind="ExternalInput").ap()
    qkv_b = nc.dram_tensor("qkv_b", [1, 3 * DM], F32, kind="ExternalInput").ap()
    cos4 = nc.dram_tensor("cos4", [S, 256], BF16, kind="ExternalInput").ap()
    sin4 = nc.dram_tensor("sin4", [S, 256], BF16, kind="ExternalInput").ap()
    ident = nc.dram_tensor("ident", [128, 128], BF16, kind="ExternalInput").ap()
    Qt = nc.dram_tensor("Qt", [32, S, 128], BF16, kind="ExternalOutput").ap()
    Kt = nc.dram_tensor("Kt", [32, S, 128], BF16, kind="ExternalOutput").ap()
    Vt = nc.dram_tensor("Vt", [32, S, 128], BF16, kind="ExternalOutput").ap()

    buf = alloc_qkv_bufs(nc)
    ctx = Ctx(nc)
    sems = emit_qkv_init(ctx, buf, ident, cos4, sin4)

    emit_ln(ctx, x_in, buf["xlnT"], buf, sems)
    segs = [
        dict(w=qkv_w, b=qkv_b, nfo=8, fo0=0, kind="q", out=Qt, h0=0),
        dict(w=qkv_w, b=qkv_b, nfo=8, fo0=8, kind="k", out=Kt, h0=0),
        dict(w=qkv_w, b=qkv_b, nfo=8, fo0=16, kind="v", out=Vt, h0=0),
    ]
    state = {"qkv_it": 0, "qkv_chunk": 0}
    emit_qkv(ctx, segs, buf["xlnT"], buf, sems, state)
    ctx.waits("sync", [(sems["st_out"], sems["st_out"].n)])
    return ctx.finalize()



def host_consts():
    pos = np.arange(S, dtype=np.float64)
    inv = 1.0 / (10000.0 ** (np.arange(64, dtype=np.float64) / 64.0))
    ang = pos[:, None] * inv[None, :]
    cos = np.cos(ang)
    sin = np.sin(ang)
    cos4 = np.tile(cos, (1, 4)).astype(BF)
    sin4 = np.tile(sin, (1, 4)).astype(BF)
    ident = np.eye(128, dtype=BF)
    m128 = np.where(np.arange(128)[None, :] > np.arange(128)[:, None],
                    np.float32(-66504.0), np.float32(0.0)).astype(BF)
    return dict(cos4=cos4, sin4=sin4, ident=ident, m128=m128)





def emit_attn(ctx, H, kvg, Qt, Kt, Vt, ctxT_out, buf, sems):
    (qtok, ktok, vtok, qfeat, kfeat, ebuf, pbuf, ptS, ctxsb, nmax, rs, rinv,
     identbf, m128_sb, ps_s, pst, ctxps) = (
        buf["qtok"], buf["ktok"], buf["vtok"], buf["qfeat"], buf["kfeat"],
        buf["ebuf"], buf["pbuf"], buf["ptS"], buf["ctxsb"], buf["nmax"],
        buf["rs"], buf["rinv"], buf["identbf"], buf["m128_sb"], buf["ps_s"],
        buf["pst"], buf["ctxps"])
    ld_q, t_tr2, s_ptv, t_s, v_max, s_exp, v_nrm, t_pv, s_cv, st_o = (
        sems["ld_q"], sems["t_tr2"], sems["s_ptv"], sems["t_s"],
        sems["v_max"], sems["s_exp"], sems["v_nrm"], sems["t_pv"],
        sems["s_cv"], sems["st_o"])
    state = sems.setdefault("_state", {})
    ps_flat = ps_s.rearrange("p a b -> p (a b)")

    for h in range(H):
        hb = h % 2
        g = h // kvg
        new_kv = (h % kvg == 0)
        gb = g % 2
        wq = [state[("q", hb)]] if ("q", hb) in state else []
        ctx.op("sync", lambda e, h=h, hb=hb: e.dma_start(
            qtok[:, hb], Qt[h].rearrange("(t p) d -> p t d", p=128)),
            waits=wq, inc=ld_q, inc_by=16)
        if new_kv:
            wk = [state[("k", gb)]] if ("k", gb) in state else []
            ctx.op("sync", lambda e, g=g, gb=gb: e.dma_start(
                ktok[:, gb], Kt[g].rearrange("(t p) d -> p t d", p=128)),
                waits=wk, inc=ld_q, inc_by=16)
            wv = [state[("v", gb)]] if ("v", gb) in state else []
            ctx.op("sync", lambda e, g=g, gb=gb: e.dma_start(
                vtok[:, gb], Vt[g].rearrange("(t p) d -> p t d", p=128)),
                waits=wv, inc=ld_q, inc_by=16)
        ld_target = ld_q.n
        def tr_batches(src_tok, dst_feat, tiles):
            for b0 in range(0, tiles, 8):
                n = t_tr2.n
                for j in range(8):
                    kt = b0 + j
                    w = []
                    if j == 0:
                        w.append((ld_q, ld_target))
                        if n >= 2:
                            w.append((s_ptv, n - 1))
                    ctx.op("tensor", lambda e, n=n, j=j, kt=kt, s=src_tok:
                           e.transpose(pst[:, n % 2, 128 * j:128 * j + 128],
                                       s[:, kt, :], identbf[:, :]),
                           waits=w, inc=(t_tr2 if j == 7 else None))
                ctx.op("scalar", lambda e, n=n, b0=b0, d=dst_feat:
                       e.copy(d[:, b0:b0 + 8, :],
                              pst[:, n % 2, :].rearrange("p (r t) -> p r t", r=8)),
                       waits=[(t_tr2, t_tr2.n)], inc=s_ptv)
        tr_batches(qtok[:, hb], qfeat, 16)
        if new_kv:
            tr_batches(ktok[:, gb], kfeat, 16)
        state[("q", hb)] = (t_tr2, t_tr2.n)
        if new_kv:
            state[("k", gb)] = (t_tr2, t_tr2.n)
        qf = qfeat.rearrange("p a b -> p (a b)")
        kf = kfeat.rearrange("p a b -> p (a b)")

        for qt in range(16):
            nk = 128 * (qt + 1)
            nseg = (nk + 511) // 512
            eb = qt % 2
            for s in range(nseg):
                w0 = s * 512
                wdt = min(512, nk - w0)
                diag = (s == nseg - 1)
                w = []
                if s == 0:
                    w.append((s_exp, s_exp.n))
                    w.append((s_ptv, s_ptv.n))
                ctx.op("tensor", lambda e, s=s, w0=w0, wdt=wdt, qt=qt, diag=diag:
                       e.matmul(ps_s[:, s, 0:wdt], qf[:, 128 * qt:128 * qt + 128],
                                kf[:, w0:w0 + wdt], start=True, stop=not diag),
                       waits=w, inc=(None if diag else None))
            off = (128 * qt) % 512
            ctx.op("tensor", lambda e, s=nseg - 1, off=off:
                   e.matmul(ps_s[:, s, off:off + 128], identbf[:, :],
                            m128_sb[:, :], start=False, stop=True),
                   inc=t_s)
            ctx.op("vector", lambda e, nk=nk: e.reduce_max(
                nmax[:, :], ps_flat[:, 0:nk], axis=AX.X, negate=True),
                waits=[(t_s, t_s.n), (s_exp, s_exp.n)], inc=v_max)
            ctx.op("scalar", lambda e, nk=nk, eb=eb: e.activation(
                ebuf[:, eb, 0:nk], ps_flat[:, 0:nk], AF.Exp,
                bias=nmax[:, :], accum_out=rs[:, :]),
                waits=[(v_max, v_max.n)], inc=s_exp)
            ctx.op("vector", lambda e: e.reciprocal(rinv[:, :], rs[:, :]),
                   waits=[(s_exp, s_exp.n)])
            ctx.op("vector", lambda e: e.drain())
            ctx.op("vector", lambda e, nk=nk, eb=eb: e.tensor_scalar_mul(
                pbuf[:, eb, 0:nk], ebuf[:, eb, 0:nk], rinv[:, :]), inc=v_nrm)
            ntile = qt + 1
            for b0 in range(0, ntile, 8):
                n = t_tr2.n
                nb = min(8, ntile - b0)
                for j in range(nb):
                    kt = b0 + j
                    w = []
                    if j == 0:
                        w.append((v_nrm, v_nrm.n))
                        if n >= 2:
                            w.append((s_ptv, n - 1))
                    ctx.op("tensor", lambda e, n=n, j=j, kt=kt, eb=eb:
                           e.transpose(pst[:, n % 2, 128 * j:128 * j + 128],
                                       pbuf[:, eb, 128 * kt:128 * kt + 128],
                                       identbf[:, :]),
                           waits=w, inc=(t_tr2 if j == nb - 1 else None))
                ctx.op("scalar", lambda e, n=n, b0=b0, nb=nb:
                       e.copy(ptS[:, b0:b0 + nb, :],
                              pst[:, n % 2, 0:128 * nb].rearrange(
                                  "p (r t) -> p r t", r=nb)),
                       waits=[(t_tr2, t_tr2.n)], inc=s_ptv)
            cb = qt % 2
            for kt in range(ntile):
                w = []
                if kt == 0:
                    w.append((s_ptv, s_ptv.n))
                    if ("ctx", cb) in state:
                        w.append(state[("ctx", cb)])
                ctx.op("tensor", lambda e, kt=kt, cb=cb, gb=gb: e.matmul(
                    ctxps[:, cb, 0:128], vtok[:, gb, kt, :], ptS[:, kt, :],
                    start=(kt == 0), stop=(kt == ntile - 1)),
                    waits=w, inc=(t_pv if kt == ntile - 1 else None))
            state[("ctx", cb)] = (s_cv, s_cv.n + 1)
            ctx.op("scalar", lambda e, qt=qt, cb=cb, hb=hb: e.copy(
                ctxsb[:, hb, 128 * qt:128 * qt + 128], ctxps[:, cb, 0:128]),
                waits=[(t_pv, t_pv.n)], inc=s_cv)
        state[("v", gb)] = (t_pv, t_pv.n)
        wst = [(s_cv, s_cv.n)]
        if ("csb", hb) in state:
            wst.append(state[("csb", hb)])
        state[("csb", hb)] = (st_o, st_o.n + 16)
        ctx.op("sync", lambda e, h=h, hb=hb: e.dma_start(
            ctxT_out[h], ctxsb[:, hb, :]), waits=wst, inc=st_o, inc_by=16)


ATTN_SEMS = ("ld_q", "t_tr2", "s_ptv", "t_s", "v_max", "s_exp", "v_nrm",
             "t_pv", "s_cv", "st_o")


def alloc_attn_bufs(nc):
    b = {}
    b["qtok"] = nc.alloc_sbuf_tensor("qtok", [128, 2, 16, 128], BF16)
    b["ktok"] = nc.alloc_sbuf_tensor("ktok", [128, 2, 16, 128], BF16)
    b["vtok"] = nc.alloc_sbuf_tensor("vtok", [128, 2, 16, 128], BF16)
    b["qfeat"] = nc.alloc_sbuf_tensor("qfeat", [128, 16, 128], BF16)
    b["kfeat"] = nc.alloc_sbuf_tensor("kfeat", [128, 16, 128], BF16)
    b["ebuf"] = nc.alloc_sbuf_tensor("ebuf", [128, 2, 2048], BF16)
    b["pbuf"] = nc.alloc_sbuf_tensor("pbuf", [128, 2, 2048], BF16)
    b["ptS"] = nc.alloc_sbuf_tensor("ptS", [128, 16, 128], BF16)
    b["ctxsb"] = nc.alloc_sbuf_tensor("ctxsb", [128, 2, 2048], BF16)
    b["nmax"] = nc.alloc_sbuf_tensor("nmax", [128, 1], F32)
    b["rs"] = nc.alloc_sbuf_tensor("rs", [128, 1], F32)
    b["rinv"] = nc.alloc_sbuf_tensor("rinv", [128, 1], F32)
    b["identbf"] = nc.alloc_sbuf_tensor("identbf", [128, 128], BF16)
    b["m128_sb"] = nc.alloc_sbuf_tensor("m128_sb", [128, 128], BF16)
    b["ps_s"] = nc.alloc_psum_tensor("ps_s", [128, 4, 512], F32)
    b["pst"] = nc.alloc_psum_tensor("pst", [128, 2, 1024], BF16)
    b["ctxps"] = nc.alloc_psum_tensor("ctxps", [128, 2, 512], F32)
    return b


def build_attn(H, kvg):
    nc = bass.Bass("TRN2", target_bir_lowering=False)
    Hk = H // kvg
    Qt = nc.dram_tensor("Qt", [H, S, 128], BF16, kind="ExternalInput").ap()
    Kt = nc.dram_tensor("Kt", [Hk, S, 128], BF16, kind="ExternalInput").ap()
    Vt = nc.dram_tensor("Vt", [Hk, S, 128], BF16, kind="ExternalInput").ap()
    ident = nc.dram_tensor("ident", [128, 128], BF16, kind="ExternalInput").ap()
    m128 = nc.dram_tensor("m128", [128, 128], BF16, kind="ExternalInput").ap()
    ctxT = nc.dram_tensor("ctxT", [H, 128, S], BF16, kind="ExternalOutput").ap()

    buf = alloc_attn_bufs(nc)
    ctx = Ctx(nc)
    sems = {k: ctx.sem(k) for k in ATTN_SEMS}
    sems["ld_c"] = ctx.sem("ld_c")
    ctx.op("sync", lambda e: e.dma_start(buf["identbf"][:, :], ident),
           inc=sems["ld_c"], inc_by=16)
    ctx.op("sync", lambda e: e.dma_start(buf["m128_sb"][:, :], m128),
           inc=sems["ld_c"], inc_by=16)
    ctx.waits("tensor", [(sems["ld_c"], 32)])
    emit_attn(ctx, H, kvg, Qt, Kt, Vt, ctxT, buf, sems)
    ctx.waits("sync", [(sems["st_o"], sems["st_o"].n)])
    return ctx.finalize()



def emit_proj_resid(ctx, nr, lhsT_sb, w_ap, b_ap, xold_ap, xnew_ap, buf, sems,
                    state, extra=None):
    (ps, wring, bias_bf, ones1, xo, xn) = (
        buf["ps"], buf["wring"], buf["bias_bf"], buf["ones1"], buf["xo"],
        buf["xn"])
    g_w, t_mm, t_bank, s_ev, st_out, g_b, ld_xo, init = (
        sems["g_w"], sems["t_mm"], sems["t_bank"], sems["s_ev"],
        sems["st_out"], sems["g_b"], sems["ld_xo"], sems["init"])

    NC = w_ap.shape[1]
    NS = wring.shape[1]
    nfo = NC // 512
    for half in (0, 1):
        for f in range(nfo):
            it = state["pr_it"]
            state["pr_it"] += 1
            c0 = f * 512
            bb = it % 2
            ctx.op("gpsimd",
                   lambda e, bb=bb, c0=c0: e.dma_start(
                       bias_bf[:, bb, :], b_ap[:, c0:c0 + 512]),
                   waits=[(t_bank, (it - 1) * 8)], inc=g_b, inc_by=16)
            for r in range(nr):
                cg = state["pr_chunk"]
                state["pr_chunk"] += 1
                slot = cg % NS
                ctx.op("gpsimd",
                       lambda e, slot=slot, r=r, c0=c0: e.dma_start(
                           wring[:, slot, :],
                           w_ap[128 * r:128 * r + 128, c0:c0 + 512]),
                       waits=[(t_mm, cg - NS + 1)], inc=g_w, inc_by=16)
                for tt in range(8):
                    w = [(g_w, g_w.n)] if tt == 0 else []
                    if r == 0:
                        if it >= 1:
                            w.append((s_ev, (it - 1) * 8 + tt + 1))
                        if it == 0 and tt == 0 and "pr_ready" in state:
                            w.append(state["pr_ready"])
                    tok = half * 1024 + tt * 128
                    ctx.op("tensor",
                           lambda e, slot=slot, r=r, tt=tt, tok=tok:
                           e.matmul(ps[:, tt, :],
                                    lhsT_sb[:, r, tok:tok + 128],
                                    wring[:, slot, :],
                                    start=(r == 0), stop=False),
                           waits=w, inc=(t_mm if tt == 7 else None))
            for tt in range(8):
                w = [(g_b, g_b.n), (init, init.n)] if tt == 0 else []
                ctx.op("tensor",
                       lambda e, tt=tt, bb=bb: e.matmul(
                           ps[:, tt, :], ones1[:, :], bias_bf[:, bb, :],
                           start=False, stop=True),
                       waits=w, inc=t_bank)
            for tt in range(8):
                tok = half * 1024 + tt * 128
                xb = tt % 2
                w = [state[("xo", xb)]] if ("xo", xb) in state else []
                ctx.op("sync",
                       lambda e, xb=xb, tok=tok, c0=c0: e.dma_start(
                           xo[:, xb, :], xold_ap[tok:tok + 128, c0:c0 + 512]),
                       waits=w, inc=ld_xo, inc_by=16)
                state[("xo", xb)] = (s_ev, s_ev.n + 1)
                w = [(t_bank, it * 8 + tt + 1), (ld_xo, ld_xo.n)]
                if ("xn", xb) in state:
                    w.append(state[("xn", xb)])
                ctx.op("vector",
                       lambda e, tt=tt, xb=xb: e.tensor_add(
                           xn[:, xb, :], ps[:, tt, :], xo[:, xb, :]),
                       waits=w, inc=s_ev)
                state[("xn", xb)] = (st_out, st_out.n + 16)
                ctx.op("sync",
                       lambda e, xb=xb, tok=tok, c0=c0: e.dma_start(
                           xnew_ap[tok:tok + 128, c0:c0 + 512], xn[:, xb, :]),
                       waits=[(s_ev, s_ev.n)], inc=st_out, inc_by=16)


PROJ_SEMS = ("g_w", "t_mm", "t_bank", "s_ev", "st_out", "g_b", "ld_xo",
             "init", "ld_c")


def build_outproj():
    nc = bass.Bass("TRN2", target_bir_lowering=False)
    ctxT = nc.dram_tensor("ctxT", [32, 128, S], BF16, kind="ExternalInput").ap()
    w = nc.dram_tensor("w", [DM, DM], F32, kind="ExternalInput").ap()
    b = nc.dram_tensor("b", [1, DM], F32, kind="ExternalInput").ap()
    x_old = nc.dram_tensor("x_old", [S, DM], F32, kind="ExternalInput").ap()
    x_new = nc.dram_tensor("x_new", [S, DM], F32, kind="ExternalOutput").ap()

    buf = {}
    buf["cx"] = nc.alloc_sbuf_tensor("cx", [128, 32, 2048], BF16)
    buf["wring"] = nc.alloc_sbuf_tensor("wring", [128, 8, 512], BF16)
    buf["bias_bf"] = nc.alloc_sbuf_tensor("bias_bf", [1, 2, 512], BF16)
    buf["ones1"] = nc.alloc_sbuf_tensor("ones1", [1, 128], BF16)
    buf["xo"] = nc.alloc_sbuf_tensor("xo", [128, 2, 512], F32)
    buf["xn"] = nc.alloc_sbuf_tensor("xn", [128, 2, 512], F32)
    buf["ps"] = nc.alloc_psum_tensor("ps", [128, 8, 512], F32)

    ctx = Ctx(nc)
    sems = {k: ctx.sem(k) for k in PROJ_SEMS}
    ld_c = sems["ld_c"]
    for h in range(32):
        ctx.op("sync", lambda e, h=h: e.dma_start(buf["cx"][:, h, :], ctxT[h]),
               inc=ld_c, inc_by=16)
    ctx.op("vector", lambda e: e.memset(buf["ones1"][:, :], 1.0),
           inc=sems["init"])
    state = {"pr_it": 0, "pr_chunk": 0, "pr_ready": (ld_c, 32 * 16)}
    emit_proj_resid(ctx, 32, buf["cx"], w, b, x_old, x_new, buf, sems, state)
    ctx.waits("sync", [(sems["st_out"], sems["st_out"].n)])
    return ctx.finalize()



def emit_moe(ctx, x_ap, xnew_ap, rw_ap, rb_ap, eb_ap, buf, sems):
    (xt, xlb, stats, mv, stdv, rstd, negmr, eps_t, identbf, ones1,
     rw_bf, rwT, eb_bf, rb_bf, xlnL, se, scores, top8, masked_bf, wsum,
     nmax_l, rs_l, rinv_l, tkT_sb, tmp1, xn, ps) = (
        buf["xt"], buf["xlb"], buf["stats"], buf["mv"], buf["stdv"],
        buf["rstd"], buf["negmr"], buf["eps_t"], buf["identbf"], buf["ones1"],
        buf["rw_bf"], buf["rwT"], buf["eb_bf"], buf["rb_bf"], buf["xlnL"],
        buf["se"], buf["scores"], buf["top8"], buf["masked_bf"], buf["wsum"],
        buf["nmax_l"], buf["rs_l"], buf["rinv_l"], buf["tkT_sb"], buf["tmp1"],
        buf["xn"], buf["ps"])
    (ld_x, v_mv, s_std, v_nm, s_xln, t_tr, s_trv, t_lg, v_sm, s_exp, v_msk,
     t_tk, s_tk, t_mx, v_ev, st_out, g_c, init) = (
        sems["ld_x"], sems["v_mv"], sems["s_std"], sems["v_nm"], sems["s_xln"],
        sems["t_tr"], sems["s_trv"], sems["t_lg"], sems["v_sm"], sems["s_exp"],
        sems["v_msk"], sems["t_tk"], sems["s_tk"], sems["t_mx"], sems["v_ev"],
        sems["st_out"], sems["g_c"], sems["init"])

    pst = ps[:, 0:2, :].bitcast(BF16)
    ps_l = ps[:, 2, 0:64]
    tkps = ps[:, 3, :].bitcast(BF16)

    ctx.op("gpsimd", lambda e: e.dma_start(rw_bf[:, :], rw_ap,
                                           max_dma_last_dim=8192),
           inc=g_c, inc_by=16)
    ctx.op("gpsimd", lambda e: e.dma_start(eb_bf[:, :], eb_ap,
                                           max_dma_last_dim=8192),
           inc=g_c, inc_by=16)
    ctx.op("gpsimd", lambda e: e.dma_start(rb_bf[:, :], rb_ap),
           inc=g_c, inc_by=16)
    for b in range(4):
        g = t_tr.n
        for j in range(8):
            r = b * 8 + j
            w = []
            if j == 0:
                w.append((g_c, 48))
                if g >= 2:
                    w.append((s_trv, g - 1))
            ctx.op("tensor", lambda e, g=g, j=j, r=r: e.transpose(
                pst[0:128, g % 2, 64 * j:64 * j + 64],
                rw_bf[:, 128 * r:128 * r + 128], identbf[0:64, 0:64]),
                waits=w, inc=(t_tr if j == 7 else None))
        ctx.op("scalar", lambda e, g=g, b=b: e.copy(
            rwT[:, 8 * b:8 * b + 8, :],
            pst[:, g % 2, 0:512].rearrange("p (r t) -> p r t", r=8)),
            waits=[(t_tr, t_tr.n)], inc=s_trv)

    for th in range(16):
        tok0 = th * 128
        ctx.op("sync", lambda e, t=tok0: e.dma_start(xt[:, :], x_ap[t:t + 128, :]),
               waits=[(v_ev, v_ev.n)], inc=ld_x, inc_by=16)
        for i in range(8):
            ctx.op("vector", lambda e, i=i: e.bn_stats(
                stats[:, 6 * i:6 * i + 6], xt[:, 512 * i:512 * i + 512]),
                waits=([(ld_x, ld_x.n)] if i == 0 else []))
        ctx.op("vector", lambda e: e.bn_aggr(mv[:, :], stats[:, :]),
               waits=[(s_std, s_std.n)], inc=v_mv)
        ctx.op("scalar", lambda e: e.activation(
            stdv[:, :], mv[:, 1:2], AF.Sqrt, bias=eps_t[:, :]),
            waits=[(v_mv, v_mv.n), (init, 2)], inc=s_std)
        ctx.op("vector", lambda e: e.reciprocal(rstd[:, :], stdv[:, :]),
               waits=[(s_std, s_std.n), (s_xln, s_xln.n)])
        ctx.op("vector", lambda e: e.drain())
        ctx.op("vector", lambda e: e.scalar_tensor_tensor(
            negmr[:, :], mv[:, 0:1], -1.0, rstd[:, :], ALU.mult, ALU.mult),
            inc=v_nm)
        ctx.op("scalar", lambda e: e.activation(
            xlb[:, :], xt[:, :], AF.Identity, bias=negmr[:, :],
            scale=rstd[:, :]),
            waits=[(v_nm, v_nm.n), (t_tr, t_tr.n)], inc=s_xln)
        for b in range(4):
            g = t_tr.n
            for j in range(8):
                r = b * 8 + j
                w = []
                if j == 0:
                    w.append((s_xln, s_xln.n))
                    if g >= 2:
                        w.append((s_trv, g - 1))
                    if b == 0 and th > 0:
                        w.append((v_ev, v_ev.n))
                ctx.op("tensor", lambda e, g=g, j=j, r=r: e.transpose(
                    pst[:, g % 2, 128 * j:128 * j + 128],
                    xlb[:, 128 * r:128 * r + 128], identbf[:, :]),
                    waits=w, inc=(t_tr if j == 7 else None))
            ctx.op("scalar", lambda e, g=g, b=b: e.copy(
                xlnL[:, 8 * b:8 * b + 8, :],
                pst[:, g % 2, :].rearrange("p (r t) -> p r t", r=8)),
                waits=[(t_tr, t_tr.n)], inc=s_trv)
        for r in range(32):
            w = [(s_trv, s_trv.n)] if r == 0 else []
            if r == 0 and th > 0:
                w.append((s_exp, s_exp.n))
                w.append((v_ev, v_ev.n))
            ctx.op("tensor", lambda e, r=r: e.matmul(
                ps_l, xlnL[:, r, :], rwT[:, r, :], start=(r == 0), stop=False),
                waits=w)
        ctx.op("tensor", lambda e: e.matmul(
            ps_l, ones1[:, :], rb_bf[:, :], start=False, stop=True),
            waits=[(g_c, 48), (init, init.n)], inc=t_lg)
        ctx.op("vector", lambda e: e.reduce_max(
            nmax_l[:, :], ps_l, axis=AX.X, negate=True),
            waits=[(t_lg, t_lg.n), (s_exp, s_exp.n)], inc=v_sm)
        ctx.op("scalar", lambda e: e.activation(
            se[:, :], ps_l, AF.Exp, bias=nmax_l[:, :], accum_out=rs_l[:, :]),
            waits=[(v_sm, v_sm.n)], inc=s_exp)
        ctx.op("vector", lambda e: e.reciprocal(rinv_l[:, :], rs_l[:, :]),
               waits=[(s_exp, s_exp.n)])
        ctx.op("vector", lambda e: e.drain())
        ctx.op("vector", lambda e: e.tensor_scalar_mul(
            scores[:, :], se[:, :], rinv_l[:, :]))
        ctx.op("vector", lambda e: e.drain())
        ctx.op("vector", lambda e: e.max(top8[:, :], scores[:, :]))
        ctx.op("vector", lambda e: e.drain())
        ctx.op("vector", lambda e: e.scalar_tensor_tensor(
            masked_bf[:, :], scores[:, :], top8[:, 7:8], scores[:, :],
            ALU.is_ge, ALU.mult))
        ctx.op("vector", lambda e: e.reduce_sum(
            wsum[:, :], top8[:, :], axis=AX.X))
        ctx.op("vector", lambda e: e.drain(), inc=v_msk)
        ctx.op("tensor", lambda e: e.transpose(
            tkps[0:64, 0:128], masked_bf[:, :], identbf[:, :]),
            waits=[(v_msk, v_msk.n)], inc=t_tk)
        ctx.op("scalar", lambda e: e.copy(tkT_sb[:, :], tkps[0:64, 0:128]),
               waits=[(t_tk, t_tk.n)], inc=s_tk)
        for sgi in range(8):
            w = [(s_tk, s_tk.n)] if sgi == 0 else []
            ctx.op("tensor", lambda e, sgi=sgi: e.matmul(
                ps[:, sgi, :], tkT_sb[:, :],
                eb_bf[:, 512 * sgi:512 * sgi + 512], start=True, stop=True),
                waits=w, inc=t_mx)
        for sgi in range(8):
            w = [(t_mx, t_mx.n), (v_msk, v_msk.n)] if sgi == 0 else []
            ctx.op("vector", lambda e, sgi=sgi: e.tensor_scalar(
                tmp1[:, :], xlb[:, 512 * sgi:512 * sgi + 512], wsum[:, :], None,
                ALU.mult), waits=w)
            ctx.op("vector", lambda e, sgi=sgi: e.tensor_add(
                tmp1[:, :], tmp1[:, :], xt[:, 512 * sgi:512 * sgi + 512]))
            xb = sgi % 2
            wv = [state_xn] if (state_xn := sems.get("_xn_%d" % xb)) else []
            ctx.op("vector", lambda e, sgi=sgi, xb=xb: e.tensor_add(
                xn[:, xb, :], tmp1[:, :], ps[:, sgi, :]), waits=wv, inc=v_ev)
            sems["_xn_%d" % xb] = (st_out, st_out.n + 16)
            ctx.op("sync", lambda e, sgi=sgi, xb=xb, t=tok0: e.dma_start(
                xnew_ap[t:t + 128, 512 * sgi:512 * sgi + 512], xn[:, xb, :]),
                waits=[(v_ev, v_ev.n)], inc=st_out, inc_by=16)


MOE_SEMS = ("ld_x", "v_mv", "s_std", "v_nm", "s_xln", "t_tr", "s_trv",
            "t_lg", "v_sm", "s_exp", "v_msk", "t_tk", "s_tk", "t_mx",
            "v_ev", "st_out", "g_c", "init", "ld_c")


def build_moe():
    nc = bass.Bass("TRN2", target_bir_lowering=False)
    x_old = nc.dram_tensor("x_old", [S, DM], F32, kind="ExternalInput").ap()
    rw = nc.dram_tensor("rw", [64, DM], F32, kind="ExternalInput").ap()
    rb = nc.dram_tensor("rb", [1, 64], F32, kind="ExternalInput").ap()
    eb = nc.dram_tensor("eb", [64, DM], F32, kind="ExternalInput").ap()
    ident = nc.dram_tensor("ident", [128, 128], BF16, kind="ExternalInput").ap()
    x_new = nc.dram_tensor("x_new", [S, DM], F32, kind="ExternalOutput").ap()

    buf = {}
    buf["xt"] = nc.alloc_sbuf_tensor("xt", [128, DM], F32)
    buf["xlb"] = nc.alloc_sbuf_tensor("xlb", [128, DM], BF16)
    buf["stats"] = nc.alloc_sbuf_tensor("stats", [128, 48], F32)
    buf["mv"] = nc.alloc_sbuf_tensor("mv", [128, 2], F32)
    buf["stdv"] = nc.alloc_sbuf_tensor("stdv", [128, 1], F32)
    buf["rstd"] = nc.alloc_sbuf_tensor("rstd", [128, 1], F32)
    buf["negmr"] = nc.alloc_sbuf_tensor("negmr", [128, 1], F32)
    buf["eps_t"] = nc.alloc_sbuf_tensor("eps_t", [128, 1], F32)
    buf["identbf"] = nc.alloc_sbuf_tensor("identbf", [128, 128], BF16)
    buf["ones1"] = nc.alloc_sbuf_tensor("ones1", [1, 128], BF16)
    buf["rw_bf"] = nc.alloc_sbuf_tensor("rw_bf", [64, DM], BF16)
    buf["rwT"] = nc.alloc_sbuf_tensor("rwT", [128, 32, 64], BF16)
    buf["eb_bf"] = nc.alloc_sbuf_tensor("eb_bf", [64, DM], BF16)
    buf["rb_bf"] = nc.alloc_sbuf_tensor("rb_bf", [1, 64], BF16)
    buf["xlnL"] = nc.alloc_sbuf_tensor("xlnL", [128, 32, 128], BF16)
    buf["se"] = nc.alloc_sbuf_tensor("se", [128, 64], F32)
    buf["scores"] = nc.alloc_sbuf_tensor("scores", [128, 64], F32)
    buf["top8"] = nc.alloc_sbuf_tensor("top8", [128, 8], F32)
    buf["masked_bf"] = nc.alloc_sbuf_tensor("masked_bf", [128, 64], BF16)
    buf["wsum"] = nc.alloc_sbuf_tensor("wsum", [128, 1], F32)
    buf["nmax_l"] = nc.alloc_sbuf_tensor("nmax_l", [128, 1], F32)
    buf["rs_l"] = nc.alloc_sbuf_tensor("rs_l", [128, 1], F32)
    buf["rinv_l"] = nc.alloc_sbuf_tensor("rinv_l", [128, 1], F32)
    buf["tkT_sb"] = nc.alloc_sbuf_tensor("tkT_sb", [64, 128], BF16)
    buf["tmp1"] = nc.alloc_sbuf_tensor("tmp1", [128, 512], F32)
    buf["xn"] = nc.alloc_sbuf_tensor("xn", [128, 2, 512], F32)
    buf["ps"] = nc.alloc_psum_tensor("ps", [128, 8, 512], F32)

    ctx = Ctx(nc)
    sems = {k: ctx.sem(k) for k in MOE_SEMS}
    ctx.op("sync", lambda e: e.dma_start(buf["identbf"][:, :], ident),
           inc=sems["ld_c"], inc_by=16)
    ctx.op("vector", lambda e: e.memset(buf["eps_t"][:, :], 1e-5),
           inc=sems["init"])
    ctx.op("vector", lambda e: e.memset(buf["ones1"][:, :], 1.0),
           inc=sems["init"])
    ctx.waits("tensor", [(sems["ld_c"], 16), (sems["init"], 2)])
    emit_moe(ctx, x_old, x_new, rw, rb, eb, buf, sems)
    ctx.waits("sync", [(sems["st_out"], sems["st_out"].n)])
    return ctx.finalize()


def build_l1_qkv():
    nc = bass.Bass("TRN2", target_bir_lowering=False)
    x_in = nc.dram_tensor("x_in", [S, DM], F32, kind="ExternalInput").ap()
    Wq = nc.dram_tensor("Wq", [DM, DM], F32, kind="ExternalInput").ap()
    bq = nc.dram_tensor("bq", [1, DM], F32, kind="ExternalInput").ap()
    Wk = nc.dram_tensor("Wk", [DM, 1024], F32, kind="ExternalInput").ap()
    bk = nc.dram_tensor("bk", [1, 1024], F32, kind="ExternalInput").ap()
    Wv = nc.dram_tensor("Wv", [DM, 1024], F32, kind="ExternalInput").ap()
    bv = nc.dram_tensor("bv", [1, 1024], F32, kind="ExternalInput").ap()
    cos4 = nc.dram_tensor("cos4", [S, 256], BF16, kind="ExternalInput").ap()
    sin4 = nc.dram_tensor("sin4", [S, 256], BF16, kind="ExternalInput").ap()
    ident = nc.dram_tensor("ident", [128, 128], BF16, kind="ExternalInput").ap()
    Qt = nc.dram_tensor("Qt", [32, S, 128], BF16, kind="ExternalOutput").ap()
    Kt = nc.dram_tensor("Kt", [8, S, 128], BF16, kind="ExternalOutput").ap()
    Vt = nc.dram_tensor("Vt", [8, S, 128], BF16, kind="ExternalOutput").ap()

    buf = alloc_qkv_bufs(nc)
    ctx = Ctx(nc)
    sems = emit_qkv_init(ctx, buf, ident, cos4, sin4)

    emit_ln(ctx, x_in, buf["xlnT"], buf, sems)
    segs = [
        dict(w=Wq, b=bq, nfo=8, fo0=0, kind="q", out=Qt, h0=0),
        dict(w=Wk, b=bk, nfo=2, fo0=0, kind="k", out=Kt, h0=0),
        dict(w=Wv, b=bv, nfo=2, fo0=0, kind="v", out=Vt, h0=0),
    ]
    state = {"qkv_it": 0, "qkv_chunk": 0}
    emit_qkv(ctx, segs, buf["xlnT"], buf, sems, state)
    ctx.waits("sync", [(sems["st_out"], sems["st_out"].n)])
    return ctx.finalize()



def emit_lowrank_ffn(ctx, x_ap, y_ap, U1, V1, U2, V2, b1gT_ap, b1uT_ap, b2_ap,
                     buf, sems, dbg=None, stages=(1, 2, 3)):
    nc = ctx.nc
    ps = buf["ps"]
    xlnT = buf["xlnT"]
    PT_sb = buf["PT_sb"]

    def act(ci, c0=0, w=512):
        return xlnT[:, ci // 4, 512 * (ci % 4) + c0:512 * (ci % 4) + c0 + w]

    def zsb(tc, c0=0, w=1024):
        return xlnT[:, 24 + tc // 2, 1024 * (tc % 2) + c0:1024 * (tc % 2) + c0 + w]

    def zt(rc, c0=0, w=512):
        return xlnT[:, 26 + rc // 4, 512 * (rc % 4) + c0:512 * (rc % 4) + c0 + w]
    u1ring = buf["u1ring"]
    v1g = buf["v1g"]
    v1u = buf["v1u"]
    u2r = buf["u2r"]
    v2r = buf["v2r"]
    b2r = buf["b2r"]
    sg = buf["sg"]
    b1gT = buf["b1gT"]
    b1uT = buf["b1uT"]
    xo = buf["xo"]
    xn = buf["xn"]
    ones1 = buf["ones1"]
    identbf = buf["identbf"]
    pst = ps[:, 0:2, :].bitcast(BF16)

    (s_trv, t_tr, init) = (sems["s_trv"], sems["t_tr"], sems["init"])
    (g_u1, t_u1, s_pt, g_v1, t_gu, s_silu, v_act, g_u2, t_zc, t_z, s_zev,
     t_zt, s_ztv, g_v2, g_b2, t_v2, t_v2h, ld_xo, v_out, st_y, ld_b1) = (
        sems[k] for k in ("g_u1", "t_u1", "s_pt", "g_v1", "t_gu", "s_silu",
                          "v_act", "g_u2", "t_zc", "t_z", "s_zev", "t_zt",
                          "s_ztv", "g_v2", "g_b2", "t_v2", "t_v2h", "ld_xo",
                          "v_out", "st_y", "ld_b1"))

    ctx.op("sync", lambda e: e.dma_start(b1gT[:, :], b1gT_ap), inc=ld_b1,
           inc_by=16)
    ctx.op("sync", lambda e: e.dma_start(b1uT[:, :], b1uT_ap), inc=ld_b1,
           inc_by=16)

    ucnt = 0
    for cp in range(8):
        bank4 = (cp % 2) * 4
        ptv = None
        for r in range(32):
            slot = ucnt % 8
            ctx.op("gpsimd",
                   lambda e, slot=slot, r=r, cp=cp: e.dma_start(
                       u1ring[:, slot, :],
                       U1[128 * r:128 * r + 128, 128 * cp:128 * cp + 128]),
                   waits=[(t_u1, ucnt - 7)], inc=g_u1, inc_by=16)
            for sgi in range(4):
                w = [(g_u1, g_u1.n)] if sgi == 0 else []
                if r == 0 and sgi == 0:
                    if cp < 2:
                        w.append((s_trv, s_trv.n))
                    else:
                        w.append((s_pt, cp - 2 + 1))
                ctx.op("tensor",
                       lambda e, slot=slot, r=r, sgi=sgi, bank4=bank4: e.matmul(
                           ps[:, bank4 + sgi, :],
                           u1ring[:, slot, :],
                           xlnT[:, r, 512 * sgi:512 * sgi + 512],
                           start=(r == 0), stop=(r == 31)),
                       waits=w, inc=(t_u1 if sgi == 3 else None))
            ucnt += 1
        for sgi in range(4):
            ctx.op("scalar", lambda e, cp=cp, bank4=bank4, sgi=sgi: e.copy(
                PT_sb[:, cp, 512 * sgi:512 * sgi + 512],
                ps[:, bank4 + sgi, :]),
                waits=([(t_u1, t_u1.n)] if sgi == 0 else []),
                inc=(s_pt if sgi == 3 else None))

    if dbg is not None and "earlydump" in stages:
        ed_st = sems["st_y"]
        for cp in range(8):
            ctx.op("sync", lambda e, cp=cp: e.dma_start(
                dbg[3][cp], PT_sb[:, cp, :]),
                waits=[(s_pt, 8)], inc=ed_st, inc_by=16)
        sems["_edump"] = (ed_st, ed_st.n)

    for tg in range(4):
        t0 = 512 * tg
        for ci in range(86):
            bb = ci % 2
            w = [(t_gu, 2 * (86 * tg + ci) - 2)]
            if tg == 0 and ci == 0:
                w = [(t_tr, t_tr.n)]
            ctx.op("gpsimd", lambda e, bb=bb, ci=ci: e.dma_start(
                v1g[:, bb], V1[:, 128 * ci:128 * ci + 128].rearrange(
                    "(rc p) f -> p rc f", p=128)),
                waits=w, inc=g_v1, inc_by=16)
            ctx.op("gpsimd", lambda e, bb=bb, ci=ci: e.dma_start(
                v1u[:, bb], V1[:, 11008 + 128 * ci:11008 + 128 * ci + 128]
                .rearrange("(rc p) f -> p rc f", p=128)),
                inc=g_v1, inc_by=16)
            for rc in range(8):
                w = [(g_v1, g_v1.n)] if rc == 0 else []
                if rc == 0:
                    w.append((s_silu, s_silu.n))
                    w.append((s_pt, 8))
                    if tg == 0 and ci == 0 and "_edump" in sems:
                        w.append(sems["_edump"])
                    if tg > 0 and ci == 0 and 3 in stages:
                        w.append((s_ztv, s_ztv.n))
                ctx.op("tensor", lambda e, rc=rc, bb=bb, t0=t0: e.matmul(
                    ps[:, 0, :], v1g[:, bb, rc, :], PT_sb[:, rc, t0:t0 + 512],
                    start=(rc == 0), stop=(rc == 7)),
                    waits=w, inc=(t_gu if rc == 7 else None))
            for rc in range(8):
                w = []
                if rc == 0:
                    w.append((v_act, v_act.n))
                ctx.op("tensor", lambda e, rc=rc, bb=bb, t0=t0: e.matmul(
                    ps[:, 1, :], v1u[:, bb, rc, :], PT_sb[:, rc, t0:t0 + 512],
                    start=(rc == 0), stop=(rc == 7)),
                    waits=w, inc=(t_gu if rc == 7 else None))
            ctx.op("scalar", lambda e, ci=ci: e.activation(
                sg[:, :], ps[:, 0, :], AF.Silu, bias=b1gT[:, ci:ci + 1]),
                waits=[(t_gu, t_gu.n - 1), (ld_b1, 32), (v_act, v_act.n)],
                inc=s_silu)
            w = [(t_gu, t_gu.n), (s_silu, s_silu.n)]
            if tg == 0 and ci == 0:
                w.append((t_u1, t_u1.n))
            if tg > 0 and 2 in stages:
                w.append((t_zc, 86 * tg))
            ctx.op("vector", lambda e, ci=ci: e.scalar_tensor_tensor(
                act(ci), ps[:, 1, :], b1uT[:, ci:ci + 1], sg[:, :],
                ALU.add, ALU.mult), waits=w, inc=v_act)
        if 2 not in stages:
            continue
        for ci in range(86):
            bb = ci % 2
            ctx.op("gpsimd", lambda e, bb=bb, ci=ci: e.dma_start(
                u2r[:, bb], U2[128 * ci:128 * ci + 128, :]),
                waits=[(t_zc, 86 * tg + ci - 1)], inc=g_u2, inc_by=16)
            nmm = 0
            for tc in range(4):
                for rh in range(2):
                    w = [(g_u2, g_u2.n)] if nmm == 0 else []
                    if ci == 0 and nmm == 0:
                        w.append((s_silu, s_silu.n))
                        w.append((v_act, v_act.n))
                        if tg > 0 and 3 in stages:
                            w.append((v_out, v_out.n))
                        if 3 in stages:
                            w.append((s_zev, s_zev.n))
                    nmm += 1
                    ctx.op("tensor", lambda e, tc=tc, rh=rh, bb=bb, ci=ci:
                           e.matmul(ps[:, 2 * tc + rh, :],
                                    act(ci, 128 * tc, 128),
                                    u2r[:, bb, 512 * rh:512 * rh + 512],
                                    start=(ci == 0), stop=(ci == 85)),
                           waits=w,
                           inc=(t_zc if nmm == 8 else
                                (t_z if ci == 85 else None)))
        ctx.op("tensor", lambda e: e.nop(), waits=[(t_zc, 86 * (tg + 1))],
               inc=t_z)
        if 3 not in stages:
            continue
        for tc in range(4):
            for rh in range(2):
                ctx.op("scalar", lambda e, tc=tc, rh=rh: e.copy(
                    zsb(tc, 512 * rh, 512),
                    ps[:, 2 * tc + rh, :]),
                    waits=([(t_z, t_z.n)] if (tc == 0 and rh == 0) else []),
                    inc=(s_zev if rh == 1 else None))
        if dbg is not None and tg == 0:
            zd_, ad_ = dbg
            for tc in range(4):
                ctx.op("sync", lambda e, tc=tc: e.dma_start(
                    zd_[tc], Z_sb[:, tc, :]), waits=[(s_zev, s_zev.n)],
                    inc=sems["st_y"], inc_by=16)
            for cig in range(0, 86, 2):
                ctx.op("sync", lambda e, cig=cig: e.dma_start(
                    ad_[cig:cig + 2].rearrange("c p t -> p c t"),
                    act[:, cig:cig + 2, :]), waits=[(v_act, 86)],
                    inc=sems["st_y"], inc_by=16)
        for tc in range(4):
            n = t_zt.n
            for rc in range(8):
                w = []
                if rc == 0:
                    w.append((s_zev, s_zev.n))
                    if n >= 2:
                        w.append((s_ztv, s_ztv.n - 1))
                ctx.op("tensor", lambda e, n=n, tc=tc, rc=rc: e.transpose(
                    pst[:, n % 2, 128 * rc:128 * rc + 128],
                    zsb(tc, 128 * rc, 128), identbf[:, :]),
                    waits=w, inc=(t_zt if rc == 7 else None))
            for rc in range(8):
                ctx.op("scalar", lambda e, n=n, tc=tc, rc=rc: e.copy(
                    zt(rc, 128 * tc, 128),
                    pst[:, n % 2, 128 * rc:128 * rc + 128]),
                    waits=([(t_zt, t_zt.n)] if rc == 0 else []),
                    inc=(s_ztv if rc == 7 else None))
        for fi in range(8):
            for hf in range(2):
                w = [(t_v2, t_v2.n)] if hf == 0 else [(t_v2h, t_v2h.n)]
                ctx.op("gpsimd", lambda e, fi=fi, hf=hf: e.dma_start(
                    v2r[:, :, :],
                    V2[512 * hf:512 * hf + 512, 512 * fi:512 * fi + 512]
                    .rearrange("(rc p) f -> p rc f", p=128)),
                    waits=w, inc=g_v2, inc_by=16)
                for tc in range(4):
                    bank = 2 + tc
                    for rc in range(4):
                        w = [(g_v2, g_v2.n)] if (tc == 0 and rc == 0) else []
                        if hf == 0 and tc == 0 and rc == 0:
                            w.append((s_ztv, s_ztv.n))
                            w.append((v_out, v_out.n))
                        ctx.op("tensor", lambda e, bank=bank, rc=rc, tc=tc,
                               hf=hf: e.matmul(
                                   ps[:, bank, :],
                                   zt(4 * hf + rc, 128 * tc, 128),
                                   v2r[:, rc, :], start=(hf == 0 and rc == 0),
                                   stop=False),
                               waits=w,
                               inc=(t_v2h if (tc == 3 and rc == 3) else None))
            ctx.op("gpsimd", lambda e, fi=fi: e.dma_start(
                b2r[:, 0, :], b2_ap[:, 512 * fi:512 * fi + 512]),
                waits=[(t_v2, t_v2.n)], inc=g_b2, inc_by=16)
            for tc in range(4):
                bank = 2 + tc
                w = [(g_b2, g_b2.n), (init, init.n)] if tc == 0 else []
                ctx.op("tensor", lambda e, bank=bank: e.matmul(
                    ps[:, bank, :], ones1[:, :], b2r[:, 0, :],
                    start=False, stop=True),
                    waits=w, inc=t_v2)
                tok = t0 + 128 * tc
                xb = tc % 2
                if "noxo" in stages:
                    w = [(t_v2, t_v2.n)]
                    if "_ffnxn_s" in sems:
                        w.append(sems["_ffnxn_s"])
                    ctx.op("vector", lambda e, bank=bank: e.tensor_copy(
                        sg[:, :], ps[:, bank, :]), waits=w, inc=v_out)
                    sems["_ffnxn_s"] = (st_y, st_y.n + 16)
                    ctx.op("sync", lambda e, tok=tok, fi=fi: e.dma_start(
                        y_ap[tok:tok + 128, 512 * fi:512 * fi + 512], sg[:, :]),
                        waits=[(v_out, v_out.n)], inc=st_y, inc_by=16)
                else:
                    wx = [sems["_ffnxo_%d" % xb]] if ("_ffnxo_%d" % xb) in sems \
                        else [(s_pt, 8)]
                    ctx.op("sync", lambda e, xb=xb, tok=tok, fi=fi: e.dma_start(
                        xo[xb], x_ap[tok:tok + 128, 512 * fi:512 * fi + 512]),
                        waits=wx, inc=ld_xo, inc_by=16)
                    sems["_ffnxo_%d" % xb] = (v_out, v_out.n + 1)
                    w = [(t_v2, t_v2.n), (ld_xo, ld_xo.n)]
                    if ("_ffnxn_%d" % xb) in sems:
                        w.append(sems["_ffnxn_%d" % xb])
                    ctx.op("vector", lambda e, bank=bank, xb=xb: e.tensor_add(
                        xn[xb], ps[:, bank, :], xo[xb]),
                        waits=w, inc=v_out)
                    sems["_ffnxn_%d" % xb] = (st_y, st_y.n + 16)
                    ctx.op("sync", lambda e, xb=xb, tok=tok, fi=fi: e.dma_start(
                        y_ap[tok:tok + 128, 512 * fi:512 * fi + 512], xn[xb]),
                        waits=[(v_out, v_out.n)], inc=st_y, inc_by=16)


FFN_SEMS = ("ld_x", "v_mv", "s_std", "v_nm", "s_xln", "t_tr", "s_trv",
            "init", "ld_c", "g_u1", "t_u1", "s_pt", "g_v1", "t_gu", "s_silu",
            "v_act", "g_u2", "t_zc", "t_z", "s_zev", "t_zt", "s_ztv", "g_v2",
            "g_b2", "t_v2", "t_v2h", "ld_xo", "v_out", "st_y", "ld_b1")


def build_ffn1(dbg=False, stages=(1, 2, 3)):
    nc = bass.Bass("TRN2", target_bir_lowering=False)
    x_old = nc.dram_tensor("x_old", [S, DM], F32, kind="ExternalInput").ap()
    U1 = nc.dram_tensor("U1", [DM, 1024], F32, kind="ExternalInput").ap()
    V1 = nc.dram_tensor("V1", [1024, 22016], F32, kind="ExternalInput").ap()
    U2 = nc.dram_tensor("U2", [11008, 1024], F32, kind="ExternalInput").ap()
    V2 = nc.dram_tensor("V2", [1024, DM], F32, kind="ExternalInput").ap()
    b1g = nc.dram_tensor("b1g", [128, 86], F32, kind="ExternalInput").ap()
    b1u = nc.dram_tensor("b1u", [128, 86], F32, kind="ExternalInput").ap()
    b2 = nc.dram_tensor("b2", [1, DM], F32, kind="ExternalInput").ap()
    ident = nc.dram_tensor("ident", [128, 128], BF16, kind="ExternalInput").ap()
    y = nc.dram_tensor("y", [S, DM], F32, kind="ExternalOutput").ap()
    zd = None
    if dbg:
        zd = (nc.dram_tensor("zsbd", [4, 128, 1024], BF16,
                             kind="ExternalOutput").ap(),
              nc.dram_tensor("ztd", [8, 128, 512], BF16,
                             kind="ExternalOutput").ap(),
              nc.dram_tensor("actd", [86, 128, 512], BF16,
                             kind="ExternalOutput").ap(),
              nc.dram_tensor("ptd", [8, 128, 2048], BF16,
                             kind="ExternalOutput").ap())

    buf = {}
    buf["xlnT"] = nc.alloc_sbuf_tensor("xlnT", [128, 32, 2048], BF16)

    buf["xt"] = nc.alloc_sbuf_tensor("xt", [128, DM], F32)
    buf["xlb"] = nc.alloc_sbuf_tensor("xlb", [128, DM], BF16)
    xlbf = buf["xlb"][:, :]
    buf["v1g"] = xlbf[:, 0:2048].rearrange("p (b r f) -> p b r f", b=2, r=8)
    buf["v1u"] = xlbf[:, 2048:4096].rearrange("p (b r f) -> p b r f", b=2, r=8)

    buf["stats"] = nc.alloc_sbuf_tensor("stats", [128, 48], F32)
    buf["mv"] = nc.alloc_sbuf_tensor("mv", [128, 2], F32)
    buf["stdv"] = nc.alloc_sbuf_tensor("stdv", [128, 1], F32)
    buf["rstd"] = nc.alloc_sbuf_tensor("rstd", [128, 1], F32)
    buf["negmr"] = nc.alloc_sbuf_tensor("negmr", [128, 1], F32)
    buf["eps_t"] = nc.alloc_sbuf_tensor("eps_t", [128, 1], F32)
    buf["identbf"] = nc.alloc_sbuf_tensor("identbf", [128, 128], BF16)
    buf["ones1"] = nc.alloc_sbuf_tensor("ones1", [1, 128], BF16)
    buf["PT_sb"] = nc.alloc_sbuf_tensor("PT_sb", [128, 8, 2048], BF16)
    buf["u1ring"] = nc.alloc_sbuf_tensor("u1ring", [128, 8, 128], BF16)
    buf["u2r"] = nc.alloc_sbuf_tensor("u2r", [128, 2, 1024], BF16)
    buf["v2r"] = nc.alloc_sbuf_tensor("v2r", [128, 4, 512], BF16)
    buf["b2r"] = nc.alloc_sbuf_tensor("b2r", [1, 1, 512], BF16)
    xot = nc.alloc_sbuf_tensor("xot", [128, 2, 512], F32)
    xnt = nc.alloc_sbuf_tensor("xnt", [128, 2, 512], F32)
    buf["xo"] = (xot[:, 0, :], xot[:, 1, :])
    buf["xn"] = (xnt[:, 0, :], xnt[:, 1, :])
    buf["sg"] = nc.alloc_sbuf_tensor("sg", [128, 512], F32)
    buf["b1gT"] = nc.alloc_sbuf_tensor("b1gT", [128, 86], F32)
    buf["b1uT"] = nc.alloc_sbuf_tensor("b1uT", [128, 86], F32)

    buf["ps"] = nc.alloc_psum_tensor("ps", [128, 8, 512], F32)
    buf["tr_ps"] = buf["ps"][:, 0:2, :].bitcast(BF16)

    ctx = Ctx(nc)
    sems = {k: ctx.sem(k) for k in FFN_SEMS}
    ctx.op("sync", lambda e: e.dma_start(buf["identbf"][:, :], ident),
           inc=sems["ld_c"], inc_by=16)
    ctx.op("vector", lambda e: e.memset(buf["eps_t"][:, :], 1e-5),
           inc=sems["init"])
    ctx.op("vector", lambda e: e.memset(buf["ones1"][:, :], 1.0),
           inc=sems["init"])
    ctx.waits("tensor", [(sems["ld_c"], 16), (sems["init"], 2)])
    emit_ln(ctx, x_old, buf["xlnT"], buf, sems)
    emit_lowrank_ffn(ctx, x_old, y, U1, V1, U2, V2, b1g, b1u, b2, buf, sems,
                     dbg=None, stages=stages)
    if dbg:
        st_y = sems["st_y"]
        gates = [(st_y, st_y.n), (sems["v_act"], sems["v_act"].n),
                 (sems["t_zc"], sems["t_zc"].n), (sems["s_pt"], sems["s_pt"].n)]
        ctx.op("sync", lambda e: e.dma_start(
            zd[0][0].rearrange("a b -> a b"), buf["xlnT"][:, 24, 0:1024]) if False
            else e.nop(), waits=gates, inc=st_y)
        for cig in range(86):
            ctx.op("sync", lambda e, cig=cig: e.dma_start(
                zd[2][cig],
                buf["xlnT"][:, cig // 4, 512 * (cig % 4):512 * (cig % 4) + 512]),
                inc=st_y, inc_by=16)
        ctx.op("sync", lambda e: e.dma_start(
            zd[3].rearrange("c p t -> p c t"), buf["PT_sb"][:, :, :]),
            inc=st_y, inc_by=16)
    ctx.waits("sync", [(sems["st_y"], sems["st_y"].n)])
    return ctx.finalize()





def _shared_init(ctx, nc, ident, cos4, sin4, m128):
    sh = {}
    sh["identbf"] = nc.alloc_sbuf_tensor("sh_ident", [128, 128], BF16)
    sh["m128_sb"] = nc.alloc_sbuf_tensor("sh_m128", [128, 128], BF16)
    sh["cos4"] = cos4
    sh["sin4"] = sin4
    sh["eps_t"] = nc.alloc_sbuf_tensor("sh_eps", [128, 1], F32)
    sh["ones1"] = nc.alloc_sbuf_tensor("sh_ones", [1, 128], BF16)
    sh["ps"] = nc.alloc_psum_tensor("sh_ps", [128, 8, 512], F32)
    ld_c = ctx.sem("sh_ld_c")
    init = ctx.sem("sh_init")
    ctx.op("sync", lambda e: e.dma_start(sh["identbf"][:, :], ident),
           inc=ld_c, inc_by=16)
    ctx.op("sync", lambda e: e.dma_start(sh["m128_sb"][:, :], m128),
           inc=ld_c, inc_by=16)
    ctx.op("vector", lambda e: e.memset(sh["eps_t"][:, :], 1e-5), inc=init)
    ctx.op("vector", lambda e: e.memset(sh["ones1"][:, :], 1.0), inc=init)
    for eng in ("tensor", "vector", "scalar", "gpsimd"):
        ctx.waits(eng, [(ld_c, ld_c.n), (init, 2)])
    sh["ld_c"] = ld_c
    sh["init"] = init
    return sh


def _barrier(ctx, sem, val):
    for eng in Ctx.ENGINES:
        ctx.waits(eng, [(sem, val)])


def _g(es, nc, name, shape, dtype):
    return es.enter_context(nc.sbuf_tensor(name, shape, dtype))


QKV_PHASE_SEMS = ("ld_x", "v_mv", "s_std", "v_nm", "s_xln", "t_tr", "s_trv",
                  "g_w", "t_mm", "t_bank", "s_ev", "v_rope", "st_out", "g_b")


def _fused_qkv_phase(ctx, nc, sh, pname, x_ap, segs):
    from contextlib import ExitStack
    es = ExitStack()
    b = {}
    b["xlnT"] = _g(es, nc, pname + "_xlnT", [128, 32, 2048], BF16)
    b["xt"] = _g(es, nc, pname + "_xt", [128, DM], F32)
    b["xlb"] = _g(es, nc, pname + "_xlb", [128, DM], BF16)
    b["stats"] = _g(es, nc, pname + "_stats", [128, 48], F32)
    b["mv"] = _g(es, nc, pname + "_mv", [128, 2], F32)
    b["stdv"] = _g(es, nc, pname + "_stdv", [128, 1], F32)
    b["rstd"] = _g(es, nc, pname + "_rstd", [128, 1], F32)
    b["negmr"] = _g(es, nc, pname + "_negmr", [128, 1], F32)
    b["wring"] = _g(es, nc, pname + "_wring", [128, 8, 512], BF16)
    b["bias_bf"] = _g(es, nc, pname + "_biasbf", [1, 2, 512], BF16)
    b["qe"] = _g(es, nc, pname + "_qe", [128, 2, 512], BF16)
    b["qr"] = _g(es, nc, pname + "_qr", [128, 2, 512], BF16)
    b["rtmp"] = _g(es, nc, pname + "_rtmp", [128, 512], BF16)
    b["cosT"] = _g(es, nc, pname + "_cosT", [128, 16, 256], BF16)
    b["sinT"] = _g(es, nc, pname + "_sinT", [128, 16, 256], BF16)
    b["eps_t"] = sh["eps_t"]
    b["identbf"] = sh["identbf"]
    b["ones1"] = sh["ones1"]
    b["ps"] = sh["ps"]
    b["tr_ps"] = sh["ps"][:, 0:2, :].bitcast(BF16)
    sems = ctx.phase_sems(QKV_PHASE_SEMS)
    sems["init"] = sh["init"]
    ldt = ctx.phase_sems(("x_ldt",))["x_ldt"]
    ctx.op("sync", lambda e: e.dma_start(
        b["cosT"][:, :, :], sh["cos4"].rearrange("(t p) d -> p t d", p=128)),
        inc=ldt, inc_by=16)
    ctx.op("sync", lambda e: e.dma_start(
        b["sinT"][:, :, :], sh["sin4"].rearrange("(t p) d -> p t d", p=128)),
        inc=ldt, inc_by=16)
    ctx.waits("vector", [(ldt, 32)])
    emit_ln(ctx, x_ap, b["xlnT"], b, sems)
    state = {"qkv_it": 0, "qkv_chunk": 0}
    emit_qkv(ctx, segs, b["xlnT"], b, sems, state)
    es.close()
    return sems["st_out"]


ATTN_PHASE_SEMS = ("ld_q", "t_tr2", "s_ptv", "t_s", "v_max", "s_exp", "v_nrm",
                   "t_pv", "s_cv", "st_o")


def _fused_attn_phase(ctx, nc, sh, pname, H, kvg, Qt, Kt, Vt, ctxT):
    from contextlib import ExitStack
    es = ExitStack()
    b = {}
    b["qtok"] = _g(es, nc, pname + "_qtok", [128, 2, 16, 128], BF16)
    b["ktok"] = _g(es, nc, pname + "_ktok", [128, 2, 16, 128], BF16)
    b["vtok"] = _g(es, nc, pname + "_vtok", [128, 2, 16, 128], BF16)
    b["qfeat"] = _g(es, nc, pname + "_qfeat", [128, 16, 128], BF16)
    b["kfeat"] = _g(es, nc, pname + "_kfeat", [128, 16, 128], BF16)
    b["ebuf"] = _g(es, nc, pname + "_ebuf", [128, 2, 2048], BF16)
    b["pbuf"] = _g(es, nc, pname + "_pbuf", [128, 2, 2048], BF16)
    b["ptS"] = _g(es, nc, pname + "_ptS", [128, 16, 128], BF16)
    b["ctxsb"] = _g(es, nc, pname + "_ctxsb", [128, 2, 2048], BF16)
    b["nmax"] = _g(es, nc, pname + "_nmax", [128, 1], F32)
    b["rs"] = _g(es, nc, pname + "_rs", [128, 1], F32)
    b["rinv"] = _g(es, nc, pname + "_rinv", [128, 1], F32)
    b["identbf"] = sh["identbf"]
    b["m128_sb"] = sh["m128_sb"]
    b["ps_s"] = sh["ps"][:, 0:4, :]
    b["pst"] = sh["ps"][:, 4:6, :].bitcast(BF16)
    b["ctxps"] = sh["ps"][:, 6:8, :]
    sems = ctx.phase_sems(ATTN_PHASE_SEMS)
    emit_attn(ctx, H, kvg, Qt, Kt, Vt, ctxT, b, sems)
    es.close()
    return sems["st_o"]


PROJ_PHASE_SEMS = ("g_w", "t_mm", "t_bank", "s_ev", "st_out", "g_b", "ld_xo",
                   "ld_c")


def _fused_proj_phase(ctx, nc, sh, pname, ctxT, w_ap, b_ap, xold_ap, xnew_ap):
    from contextlib import ExitStack
    es = ExitStack()
    b = {}
    b["cx"] = _g(es, nc, pname + "_cx", [128, 32, 2048], BF16)
    b["wring"] = _g(es, nc, pname + "_wring", [128, 8, 512], BF16)
    b["bias_bf"] = _g(es, nc, pname + "_biasbf", [1, 2, 512], BF16)
    b["xo"] = _g(es, nc, pname + "_xo", [128, 2, 512], F32)
    b["xn"] = _g(es, nc, pname + "_xn", [128, 2, 512], F32)
    b["ones1"] = sh["ones1"]
    b["ps"] = sh["ps"]
    sems = ctx.phase_sems(PROJ_PHASE_SEMS)
    sems["init"] = sh["init"]
    ld_c = sems["ld_c"]
    for h in range(32):
        ctx.op("sync", lambda e, h=h: e.dma_start(b["cx"][:, h, :], ctxT[h]),
               inc=ld_c, inc_by=16)
    state = {"pr_it": 0, "pr_chunk": 0, "pr_ready": (ld_c, 32 * 16)}
    emit_proj_resid(ctx, 32, b["cx"], w_ap, b_ap, xold_ap, xnew_ap, b, sems,
                    state)
    es.close()
    return sems["st_out"]


MOE_PHASE_SEMS = ("ld_x", "v_mv", "s_std", "v_nm", "s_xln", "t_tr", "s_trv",
                  "t_lg", "v_sm", "s_exp", "v_msk", "t_tk", "s_tk", "t_mx",
                  "v_ev", "st_out", "g_c")


def _fused_moe_phase(ctx, nc, sh, pname, x_ap, xnew_ap, rw_ap, rb_ap, eb_ap):
    from contextlib import ExitStack
    es = ExitStack()
    b = {}
    b["xt"] = _g(es, nc, pname + "_xt", [128, DM], F32)
    b["xlb"] = _g(es, nc, pname + "_xlb", [128, DM], BF16)
    b["stats"] = _g(es, nc, pname + "_stats", [128, 48], F32)
    b["mv"] = _g(es, nc, pname + "_mv", [128, 2], F32)
    b["stdv"] = _g(es, nc, pname + "_stdv", [128, 1], F32)
    b["rstd"] = _g(es, nc, pname + "_rstd", [128, 1], F32)
    b["negmr"] = _g(es, nc, pname + "_negmr", [128, 1], F32)
    b["rw_bf"] = _g(es, nc, pname + "_rwbf", [64, DM], BF16)
    b["rwT"] = _g(es, nc, pname + "_rwT", [128, 32, 64], BF16)
    b["eb_bf"] = _g(es, nc, pname + "_ebbf", [64, DM], BF16)
    b["rb_bf"] = _g(es, nc, pname + "_rbbf", [1, 64], BF16)
    b["xlnL"] = _g(es, nc, pname + "_xlnL", [128, 32, 128], BF16)
    b["se"] = _g(es, nc, pname + "_se", [128, 64], F32)
    b["scores"] = _g(es, nc, pname + "_scores", [128, 64], F32)
    b["top8"] = _g(es, nc, pname + "_top8", [128, 8], F32)
    b["masked_bf"] = _g(es, nc, pname + "_maskedbf", [128, 64], BF16)
    b["wsum"] = _g(es, nc, pname + "_wsum", [128, 1], F32)
    b["nmax_l"] = _g(es, nc, pname + "_nmaxl", [128, 1], F32)
    b["rs_l"] = _g(es, nc, pname + "_rsl", [128, 1], F32)
    b["rinv_l"] = _g(es, nc, pname + "_rinvl", [128, 1], F32)
    b["tkT_sb"] = _g(es, nc, pname + "_tkT", [64, 128], BF16)
    b["tmp1"] = _g(es, nc, pname + "_tmp1", [128, 512], F32)
    b["xn"] = _g(es, nc, pname + "_xn", [128, 2, 512], F32)
    b["eps_t"] = sh["eps_t"]
    b["identbf"] = sh["identbf"]
    b["ones1"] = sh["ones1"]
    b["ps"] = sh["ps"]
    sems = ctx.phase_sems(MOE_PHASE_SEMS)
    sems["init"] = sh["init"]
    emit_moe(ctx, x_ap, xnew_ap, rw_ap, rb_ap, eb_ap, b, sems)
    es.close()
    return sems["st_out"]


FFN_PHASE_SEMS = ("ld_x", "v_mv", "s_std", "v_nm", "s_xln", "t_tr", "s_trv",
                  "g_u1", "t_u1", "s_pt", "g_v1", "t_gu", "s_silu",
                  "v_act", "g_u2", "t_zc", "t_z", "s_zev", "t_zt", "s_ztv",
                  "g_v2", "g_b2", "t_v2", "t_v2h", "ld_xo", "v_out", "st_y",
                  "ld_b1")


def _fused_ffn_phase(ctx, nc, sh, pname, x_ap, y_ap, U1, V1, U2, V2, b1g, b1u,
                     b2):
    from contextlib import ExitStack
    es = ExitStack()
    b = {}
    b["xlnT"] = _g(es, nc, pname + "_xlnT", [128, 32, 2048], BF16)
    b["xt"] = _g(es, nc, pname + "_xt", [128, DM], F32)
    b["xlb"] = _g(es, nc, pname + "_xlb", [128, DM], BF16)
    xlbf = b["xlb"][:, :]
    b["v1g"] = xlbf[:, 0:2048].rearrange("p (b r f) -> p b r f", b=2, r=8)
    b["v1u"] = xlbf[:, 2048:4096].rearrange("p (b r f) -> p b r f", b=2, r=8)
    b["stats"] = _g(es, nc, pname + "_stats", [128, 48], F32)
    b["mv"] = _g(es, nc, pname + "_mv", [128, 2], F32)
    b["stdv"] = _g(es, nc, pname + "_stdv", [128, 1], F32)
    b["rstd"] = _g(es, nc, pname + "_rstd", [128, 1], F32)
    b["negmr"] = _g(es, nc, pname + "_negmr", [128, 1], F32)
    b["PT_sb"] = _g(es, nc, pname + "_PTsb", [128, 8, 2048], BF16)
    b["u1ring"] = _g(es, nc, pname + "_u1ring", [128, 8, 128], BF16)
    b["u2r"] = _g(es, nc, pname + "_u2r", [128, 2, 1024], BF16)
    b["v2r"] = _g(es, nc, pname + "_v2r", [128, 4, 512], BF16)
    b["b2r"] = _g(es, nc, pname + "_b2r", [1, 1, 512], BF16)
    b["sg"] = _g(es, nc, pname + "_sg", [128, 512], F32)
    b["b1gT"] = _g(es, nc, pname + "_b1gT", [128, 86], F32)
    b["b1uT"] = _g(es, nc, pname + "_b1uT", [128, 86], F32)
    xot = _g(es, nc, pname + "_xot", [128, 2, 512], F32)
    xnt = _g(es, nc, pname + "_xnt", [128, 2, 512], F32)
    b["xo"] = (xot[:, 0, :], xot[:, 1, :])
    b["xn"] = (xnt[:, 0, :], xnt[:, 1, :])
    b["eps_t"] = sh["eps_t"]
    b["identbf"] = sh["identbf"]
    b["ones1"] = sh["ones1"]
    b["ps"] = sh["ps"]
    b["tr_ps"] = sh["ps"][:, 0:2, :].bitcast(BF16)
    sems = ctx.phase_sems(FFN_PHASE_SEMS)
    sems["init"] = sh["init"]
    emit_ln(ctx, x_ap, b["xlnT"], b, sems)
    emit_lowrank_ffn(ctx, x_ap, y_ap, U1, V1, U2, V2, b1g, b1u, b2, b, sems)
    es.close()
    return sems["st_y"]


def build_fused():
    nc = bass.Bass("TRN2", target_bir_lowering=False)
    I = lambda n, s, d: nc.dram_tensor(n, s, d, kind="ExternalInput").ap()
    x_in = I("x_in", [S, DM], F32)
    qkv_w = I("qkv_w", [DM, 3 * DM], F32)
    qkv_b = I("qkv_b", [1, 3 * DM], F32)
    out_w = I("out_w", [DM, DM], F32)
    out_b = I("out_b", [1, DM], F32)
    rw = I("rw", [64, DM], F32)
    rb = I("rb", [1, 64], F32)
    eb = I("eb", [64, DM], F32)
    Wq = I("Wq", [DM, DM], F32)
    bq = I("bq", [1, DM], F32)
    Wk = I("Wk", [DM, 1024], F32)
    bk = I("bk", [1, 1024], F32)
    Wv = I("Wv", [DM, 1024], F32)
    bv = I("bv", [1, 1024], F32)
    Wo = I("Wo", [DM, DM], F32)
    bo = I("bo", [1, DM], F32)
    U1 = I("U1", [DM, 1024], F32)
    V1 = I("V1", [1024, 22016], F32)
    U2 = I("U2", [11008, 1024], F32)
    V2 = I("V2", [1024, DM], F32)
    b1g = I("b1g", [128, 86], F32)
    b1u = I("b1u", [128, 86], F32)
    b2 = I("b2", [1, DM], F32)
    cos4 = I("cos4", [S, 256], BF16)
    sin4 = I("sin4", [S, 256], BF16)
    ident = I("ident", [128, 128], BF16)
    m128 = I("m128", [128, 128], BF16)
    y = nc.dram_tensor("y", [S, DM], F32, kind="ExternalOutput").ap()

    D = lambda n, s, d: nc.dram_tensor(n, s, d, kind="Internal").ap()
    Qt = D("sQt", [32, S, 128], BF16)
    Kt = D("sKt", [32, S, 128], BF16)
    Vt = D("sVt", [32, S, 128], BF16)
    ctxT = D("sctxT", [32, 128, S], BF16)
    x1 = D("sx1", [S, DM], F32)
    x2 = D("sx2", [S, DM], F32)
    x3 = D("sx3", [S, DM], F32)

    ctx = Ctx(nc)
    sh = _shared_init(ctx, nc, ident, cos4, sin4, m128)

    segs0 = [
        dict(w=qkv_w, b=qkv_b, nfo=8, fo0=0, kind="q", out=Qt, h0=0),
        dict(w=qkv_w, b=qkv_b, nfo=8, fo0=8, kind="k", out=Kt, h0=0),
        dict(w=qkv_w, b=qkv_b, nfo=8, fo0=16, kind="v", out=Vt, h0=0),
    ]
    s = _fused_qkv_phase(ctx, nc, sh, "p0", x_in, segs0)
    _barrier(ctx, s, s.n)
    s = _fused_attn_phase(ctx, nc, sh, "p1", 32, 1, Qt, Kt, Vt, ctxT)
    _barrier(ctx, s, s.n)
    s = _fused_proj_phase(ctx, nc, sh, "p2", ctxT, out_w, out_b, x_in, x1)
    _barrier(ctx, s, s.n)
    s = _fused_moe_phase(ctx, nc, sh, "p3", x1, x2, rw, rb, eb)
    _barrier(ctx, s, s.n)
    segs1 = [
        dict(w=Wq, b=bq, nfo=8, fo0=0, kind="q", out=Qt, h0=0),
        dict(w=Wk, b=bk, nfo=2, fo0=0, kind="k", out=Kt, h0=0),
        dict(w=Wv, b=bv, nfo=2, fo0=0, kind="v", out=Vt, h0=0),
    ]
    s = _fused_qkv_phase(ctx, nc, sh, "p4", x2, segs1)
    _barrier(ctx, s, s.n)
    s = _fused_attn_phase(ctx, nc, sh, "p5", 32, 4, Qt, Kt, Vt, ctxT)
    _barrier(ctx, s, s.n)
    s = _fused_proj_phase(ctx, nc, sh, "p6", ctxT, Wo, bo, x2, x3)
    _barrier(ctx, s, s.n)
    s = _fused_ffn_phase(ctx, nc, sh, "p7", x3, y, U1, V1, U2, V2, b1g, b1u,
                         b2)
    ctx.waits("sync", [(s, s.n)])
    return ctx.finalize()


_PROGS = {}


def _get_progs():
    if not _PROGS:
        _PROGS["fused"] = build_fused()
    return _PROGS


def kernel(**inputs):
    p = _get_progs()
    cc = host_consts()
    B = 8
    f32 = np.float32
    cores = list(range(B))
    x = np.ascontiguousarray(np.asarray(inputs["L0_attn_hidden_states"], f32))
    qkv_w = np.asarray(inputs["L0_attn_qkv_w"], f32)
    qkv_b = np.asarray(inputs["L0_attn_qkv_b"], f32).reshape(1, -1)
    out_w = np.asarray(inputs["L0_attn_out_w"], f32)
    out_b = np.asarray(inputs["L0_attn_out_b"], f32).reshape(1, -1)
    rw = np.asarray(inputs["L0_ffn_router_weight"], f32)
    rb = np.asarray(inputs["L0_ffn_router_bias"], f32).reshape(1, -1)
    eb = np.asarray(inputs["L0_ffn_expert_bias"], f32)
    Wq = np.asarray(inputs["L1_attn_Wq"], f32)
    bq = np.asarray(inputs["L1_attn_bq"], f32).reshape(1, -1)
    Wk = np.asarray(inputs["L1_attn_Wk"], f32)
    bk = np.asarray(inputs["L1_attn_bk"], f32).reshape(1, -1)
    Wv = np.asarray(inputs["L1_attn_Wv"], f32)
    bv = np.asarray(inputs["L1_attn_bv"], f32).reshape(1, -1)
    Wo = np.asarray(inputs["L1_attn_Wo"], f32)
    bo = np.asarray(inputs["L1_attn_bo"], f32).reshape(1, -1)
    U1 = np.asarray(inputs["L1_ffn_U1"], f32)
    V1 = np.asarray(inputs["L1_ffn_V1"], f32)
    U2 = np.asarray(inputs["L1_ffn_U2"], f32)
    V2 = np.asarray(inputs["L1_ffn_V2"], f32)
    b1 = np.asarray(inputs["L1_ffn_b1"], f32)
    b2 = np.asarray(inputs["L1_ffn_b2"], f32).reshape(1, -1)
    b1g = np.ascontiguousarray(b1[:11008].reshape(86, 128).T)
    b1u = np.ascontiguousarray(b1[11008:].reshape(86, 128).T)

    maps = [dict(x_in=np.ascontiguousarray(x[b]), qkv_w=qkv_w, qkv_b=qkv_b,
                 out_w=out_w, out_b=out_b, rw=rw, rb=rb, eb=eb, Wq=Wq, bq=bq,
                 Wk=Wk, bk=bk, Wv=Wv, bv=bv, Wo=Wo, bo=bo, U1=U1, V1=V1,
                 U2=U2, V2=V2, b1g=b1g, b1u=b1u, b2=b2, cos4=cc["cos4"],
                 sin4=cc["sin4"], ident=cc["ident"], m128=cc["m128"])
            for b in range(B)]
    res = run_bass_kernel_spmd(p["fused"], maps, core_ids=cores).results
    return np.stack([res[b]["y"] for b in range(B)], axis=0)

